# version 02
# baseline (speedup 1.0000x reference)
import numpy as np
import ml_dtypes
from contextlib import ExitStack

import concourse.bass as bass
import concourse.mybir as mybir
from concourse.bass_utils import run_bass_kernel_spmd

F32 = mybir.dt.float32
BF16 = mybir.dt.bfloat16
I32 = mybir.dt.int32
AF = mybir.ActivationFunctionType
ALU = mybir.AluOpType
AX = mybir.AxisListType
BF = ml_dtypes.bfloat16

B, N, HID = 2, 4096, 4096
H, KVH, Dh, W = 32, 8, 128, 64
GRP = H // KVH
I_FFN, R_LR = 11008, 32
NCORES = 8
T = 1024
TC = T // 512
NT = T // 128
HALO = 64
TH = T + HALO
PFB = 3
DT = HID // 128
IT = I_FFN // 128
NEG = -1.0e30
EPS = 1e-5

ENGINES = ("sync", "gpsimd", "scalar", "vector", "tensor")


class Sem:
    def __init__(self, nc, ctx, name):
        self.h = ctx.enter_context(nc.semaphore(name))
        self.n = 0

    def bump(self, k=1):
        self.n += k
        return self.n


class Prog:

    def __init__(self):
        self.q = {e: [] for e in ENGINES}

    def on(self, eng, fn):
        self.q[eng].append(fn)

    def dma(self, eng, out_ap, in_ap, sem, **kw):
        tgt = sem.bump(16)
        self.on(eng, lambda e, o=out_ap, i=in_ap, k=dict(kw): e.dma_start(out=o, in_=i, **k).then_inc(sem.h, 16))
        return tgt

    def dmat(self, eng, out_ap, in_ap, sem):
        tgt = sem.bump(16)
        self.on(eng, lambda e, o=out_ap, i=in_ap: e.dma_start_transpose(o, i).then_inc(sem.h, 16))
        return tgt

    def wait(self, eng, sem, val=None):
        v = sem.n if val is None else val
        if v > 0:
            self.on(eng, lambda e, s=sem.h, vv=v: e.wait_ge(s, vv))

    def op(self, eng, fn, sem=None, k=1):
        if sem is None:
            self.on(eng, fn)
            return None
        tgt = sem.bump(k)
        self.on(eng, lambda e, f=fn, s=sem.h, kk=k: f(e).then_inc(s, kk))
        return tgt

    def drain(self, eng):
        self.on(eng, lambda e: e.drain())

    def run(self, block):
        for name in ENGINES:
            sec = getattr(block, name)
            ops = self.q[name]

            def make(ops_):
                def body(e):
                    for f in ops_:
                        f(e)
                return body
            sec(make(ops))



def host_consts(core_id):
    b, j = divmod(core_id, 4)
    t0 = j * T
    c = {}
    c["ident_f"] = np.eye(128, dtype=np.float32)
    c["ident_b"] = np.eye(128, dtype=BF)
    jj, ii = np.meshgrid(np.arange(128), np.arange(128), indexing="ij")
    c["tri_incl"] = ((jj <= ii).astype(np.float32) / 16.0)
    c["tri_rev"] = ((jj > ii).astype(np.float32) / 16.0)
    c["mask_tri"] = (jj <= ii).astype(np.float32)
    r = np.arange(128)[:, None]
    col = np.arange(192)[None, :]
    maskA = np.where((col >= r) & (col <= r + 64), 0.0, NEG).astype(np.float32)
    maskB = np.where((col >= r) & (col <= r + 64) & (col >= 64), 0.0, NEG).astype(np.float32)
    c["mask_first"] = maskB if j == 0 else maskA
    c["mask_rest"] = maskA
    pos = np.arange(t0 - HALO, t0 + T, dtype=np.float64)
    half = np.arange(Dh // 2, dtype=np.float64)
    inv = 1.0 / (10000.0 ** (2.0 * half / Dh))
    fr = inv[:, None] * pos[None, :]
    emb = np.concatenate([fr, fr], axis=0)
    c["cos_t"] = np.cos(emb).astype(np.float32)
    c["sin_t"] = np.sin(emb).astype(np.float32)
    c["ones_b"] = np.ones((128, 1), dtype=BF)
    c["neg75"] = np.full((1, 128), -7.5, dtype=BF)
    return c


CONST_SPECS = [
    ("ident_f", [128, 128], F32), ("ident_b", [128, 128], BF16),
    ("tri_incl", [128, 128], F32), ("tri_rev", [128, 128], F32),
    ("mask_tri", [128, 128], F32), ("mask_first", [128, 192], F32),
    ("mask_rest", [128, 192], F32), ("cos_t", [128, TH], F32),
    ("sin_t", [128, TH], F32), ("ones_b", [128, 1], BF16),
    ("neg75", [1, 128], BF16),
]


def shard_inputs(inputs):
    x = inputs["L0_attn_hidden_states"]
    maps = []
    for core in range(NCORES):
        b, j = divmod(core, 4)
        t0 = j * T
        m = {}
        xm = np.zeros((TH, HID), dtype=np.float32)
        lo = max(t0 - HALO, 0)
        xm[HALO - (t0 - lo):] = x[b, lo:t0 + T]
        m["x_main"] = np.ascontiguousarray(xm)
        xp = np.zeros((PFB * T, HID), dtype=np.float32)
        if t0 > 0:
            xp[PFB * T - t0:] = x[b, 0:t0]
        m["x_prefix"] = xp
        for k, v in host_consts(core).items():
            m[k] = np.ascontiguousarray(v)
        for name in ("attn_Wq", "attn_Wk", "attn_Wv", "attn_Wo",
                     "ffn_gate_q", "ffn_gate_scale", "ffn_gate_A", "ffn_gate_B",
                     "ffn_up_q", "ffn_up_scale", "ffn_up_A", "ffn_up_B",
                     "ffn_down_q", "ffn_down_scale", "ffn_down_A", "ffn_down_B"):
            m[name] = inputs["L0_" + name]
        maps.append(m)
    return maps



def build(debug_taps=(), stop_after=99):
    nc = bass.Bass("TRN2", target_bir_lowering=False)
    ctx = ExitStack()
    d = {}
    d["x_main"] = nc.dram_tensor("x_main", [TH, HID], F32, kind="ExternalInput").ap()
    d["x_prefix"] = nc.dram_tensor("x_prefix", [PFB * T, HID], F32, kind="ExternalInput").ap()
    for name, shape, dt in CONST_SPECS:
        d[name] = nc.dram_tensor(name, shape, dt, kind="ExternalInput").ap()
    d["attn_Wq"] = nc.dram_tensor("attn_Wq", [HID, HID], F32, kind="ExternalInput").ap()
    d["attn_Wk"] = nc.dram_tensor("attn_Wk", [HID, KVH * Dh], F32, kind="ExternalInput").ap()
    d["attn_Wv"] = nc.dram_tensor("attn_Wv", [HID, KVH * Dh], F32, kind="ExternalInput").ap()
    d["attn_Wo"] = nc.dram_tensor("attn_Wo", [HID, HID], F32, kind="ExternalInput").ap()
    d["ffn_gate_q"] = nc.dram_tensor("ffn_gate_q", [I_FFN, HID], I32, kind="ExternalInput").ap()
    d["ffn_gate_scale"] = nc.dram_tensor("ffn_gate_scale", [I_FFN], F32, kind="ExternalInput").ap()
    d["ffn_gate_A"] = nc.dram_tensor("ffn_gate_A", [HID, R_LR], F32, kind="ExternalInput").ap()
    d["ffn_gate_B"] = nc.dram_tensor("ffn_gate_B", [R_LR, I_FFN], F32, kind="ExternalInput").ap()
    d["ffn_up_q"] = nc.dram_tensor("ffn_up_q", [I_FFN, HID], I32, kind="ExternalInput").ap()
    d["ffn_up_scale"] = nc.dram_tensor("ffn_up_scale", [I_FFN], F32, kind="ExternalInput").ap()
    d["ffn_up_A"] = nc.dram_tensor("ffn_up_A", [HID, R_LR], F32, kind="ExternalInput").ap()
    d["ffn_up_B"] = nc.dram_tensor("ffn_up_B", [R_LR, I_FFN], F32, kind="ExternalInput").ap()
    d["ffn_down_q"] = nc.dram_tensor("ffn_down_q", [HID, I_FFN], I32, kind="ExternalInput").ap()
    d["ffn_down_scale"] = nc.dram_tensor("ffn_down_scale", [HID], F32, kind="ExternalInput").ap()
    d["ffn_down_A"] = nc.dram_tensor("ffn_down_A", [I_FFN, R_LR], F32, kind="ExternalInput").ap()
    d["ffn_down_B"] = nc.dram_tensor("ffn_down_B", [R_LR, HID], F32, kind="ExternalInput").ap()
    d["out"] = nc.dram_tensor("out", [T, HID], F32, kind="ExternalOutput").ap()
    spill_kind = lambda nm: "ExternalOutput" if nm in debug_taps else "Internal"
    d["sp_y1"] = nc.dram_tensor("sp_y1", [T, HID], F32, kind=spill_kind("sp_y1")).ap()
    d["sp_yt"] = nc.dram_tensor("sp_yt", [HID, T], F32, kind=spill_kind("sp_yt")).ap()

    p = Prog()
    sb_ = {}

    def sb(name, shape, dt, **kw):
        t = ctx.enter_context(nc.sbuf_tensor(name, shape, dt, **kw))
        sb_[name] = t
        return t

    def ps(name, shape, dt=F32):
        t = ctx.enter_context(nc.psum_tensor(name, shape, dt))
        return t

    def dbg(name, ap, shape, dt):
        if name in debug_taps:
            o = nc.dram_tensor("dbg_" + name, shape, dt, kind="ExternalOutput").ap()
            tgt = p.dma("sync", o, ap, sm["st"])
            p.wait("sync", sm["st"], tgt)

    o_T = sb("o_T", [128, H * T], BF16)
    ar2 = sb("ar2", [128, DT * (TH + HALO)], BF16, is_transpose_dest=True)
    qTh = sb("qTh", [128, T], BF16)
    kTh = sb("kTh", [128, TH], BF16)
    skh = sb("skh", [128, TH], BF16)
    v_nt = sb("v_nt", [128, (NT + 1) * 128], BF16, is_transpose_dest=True)
    v_gla = sb("v_gla", [128, NT * 128], BF16, is_transpose_dest=True)
    vTst = sb("vTst", [128, TH + HALO], BF16)
    ks_h = sb("ks_h", [128, T], BF16)
    gs_h = sb("gs_h", [128, T], F32)
    S_f = sb("S_f", [128, KVH * Dh], F32)
    S_b = sb("S_b", [128, KVH * Dh], BF16)
    wst = sb("wst", [128, DT * 128], BF16)
    wst2 = sb("wst2", [128, DT * 128], BF16, is_transpose_dest=True)
    scr = sb("scr", [128, 1536], F32)
    scrb = sb("scrb", [128, 1024], BF16)
    red = sb("red", [128, 16], F32)
    cst = {}
    for name, shape, dt in CONST_SPECS:
        cst[name] = sb("c_" + name, shape, dt)

    sm = {k: Sem(nc, ctx, k) for k in
          ("ldw", "ldx", "ldh", "lw1", "lw2", "lda", "mm", "ev", "sc", "tr", "st", "ph", "pf")}

    def barrier():
        tgt = sm["ph"].n + 5
        for e in ENGINES:
            p.on(e, lambda en, s=sm["ph"].h: en.sem_inc(s, 1))
        sm["ph"].n = tgt
        for e in ENGINES:
            p.wait(e, sm["ph"], tgt)

    for name, shape, dt in CONST_SPECS:
        p.dma("sync", cst[name][:], d[name][:], sm["ldh"])
    for e in ENGINES:
        p.wait(e, sm["ldh"])
    p.op("vector", lambda e: e.memset(S_f[:], 0.0), sm["ev"])
    p.op("vector", lambda e: e.memset(red[:, 9:10], EPS), sm["ev"])
    p.op("vector", lambda e: e.memset(S_b[:], 0.0), sm["ev"])
    p.op("vector", lambda e: e.memset(vTst[:], 0.0), sm["ev"])
    barrier()

    def build_xT(dst, dst_stride, src_ap, rows, tag):
        ntile = (rows + 127) // 128
        for tt in range(ntile):
            r0 = tt * 128
            nr = min(128, rows - r0)
            stg = scrb
            p.dma("gpsimd", wst[0:nr, 0:4096], src_ap[r0:r0 + nr, :], sm["lw1"])
            p.wait("sync", sm["lw1"])
            if nr < 128:
                p.wait("vector", sm["lw1"])
                p.op("vector", lambda e, a=wst[nr:128, 0:4096]: e.memset(a, 0.0), sm["ev"])
                p.wait("sync", sm["ev"])
            src3 = wst[:, 0:4096].rearrange("p (dd f) -> p dd f", f=128)
            dst3 = dst[:, 0:DT * dst_stride].rearrange("p (dd t) -> p dd t", t=dst_stride)[:, :, r0:r0 + 128]
            tgt = p.dmat("sync", dst3, src3, sm["tr"])
            p.wait("gpsimd", sm["tr"], tgt)
        p.wait("tensor", sm["tr"])
        p.wait("vector", sm["tr"])

    kt_all = sb("kt_all", [128, NT * 128], BF16)
    ea_all = sb("ea_all", [128, NT * 128], F32)
    S_snap = sb("S_snap", [128, (NT + 1) * 128], BF16)
    sqh = sb("sqh", [128, T], BF16)

    pA = ps("pA", [128, 512])
    pB = ps("pB", [128, 512])
    pC = ps("pC", [128, 512])
    pD = ps("pD", [128, 512])
    pE = ps("pE", [128, 512])
    pF = ps("pF", [128, 1024], BF16)
    pG = ps("pG", [128, 512])
    pH = ps("pH", [128, 1024], BF16)

    ISQ = 1.0 / float(np.sqrt(128.0))
    XTS = TH + HALO

    def load_wstrip(dst, wd, col0, width=128, rows=HID):
        p.wait("gpsimd", sm["mm"])
        sem = sm["lw1"] if dst is wst else sm["lw2"]
        tg = None
        rchunk = 1024
        for r0 in range(0, rows, rchunk):
            src = wd[r0:r0 + rchunk, col0:col0 + width].rearrange("(dd p) o -> p dd o", p=128)
            c0 = (r0 // 128) * width
            dstc = dst[:, c0:c0 + (rchunk // 128) * width].rearrange("p (dd o) -> p dd o", o=width)
            tg = p.dma("gpsimd", dstc, src, sem)
        return tg

    def proj_strip(wdst, xts, rhs_base_off, ntok, out_cb, ld_tgt):
        p.wait("tensor", sm["lw1"] if wdst is wst else sm["lw2"], ld_tgt)
        nch = (ntok + 511) // 512
        sc_done = []
        for c in range(nch):
            t0c = c * 512
            n = min(512, ntok - t0c)
            pb_ = pA if c % 2 == 0 else pB
            if c >= 2:
                p.wait("tensor", sm["sc"], sc_done[c - 2])
            for dd in range(DT):
                lhs_ap = wdst[:, dd * 128:(dd + 1) * 128]
                rhs_ap = ar2[:, dd * xts + rhs_base_off + t0c: dd * xts + rhs_base_off + t0c + n]
                p.op("tensor",
                     lambda e, pb2=pb_, L=lhs_ap, R=rhs_ap, s=(dd == 0), st=(dd == DT - 1), n2=n:
                     e.matmul(pb2[:, 0:n2], L, R, start=s, stop=st),
                     sem=sm["mm"] if dd == DT - 1 else None)
            out_cb(pb_, c, t0c, n)
            sc_done.append(sm["sc"].n)

    def rope_emit(dst, srcF, c0, ncols, tab0):
        co, si = cst["cos_t"], cst["sin_t"]
        sl = slice(c0, c0 + ncols)
        tl = slice(tab0, tab0 + ncols)
        seq = [
            lambda e: e.tensor_copy(dst[0:64, sl], srcF[64:128, sl]),
            lambda e: e.tensor_mul(scr[0:64, 0:ncols], dst[0:64, sl], si[0:64, tl]),
            lambda e: e.tensor_mul(dst[0:64, sl], srcF[0:64, sl], co[0:64, tl]),
            lambda e: e.tensor_sub(dst[0:64, sl], dst[0:64, sl], scr[0:64, 0:ncols]),
            lambda e: e.tensor_copy(dst[64:128, sl], srcF[0:64, sl]),
            lambda e: e.tensor_mul(scr[64:128, 0:ncols], dst[64:128, sl], si[64:128, tl]),
            lambda e: e.tensor_mul(dst[64:128, sl], srcF[64:128, sl], co[64:128, tl]),
            lambda e: e.tensor_add(dst[64:128, sl], dst[64:128, sl], scr[64:128, 0:ncols]),
        ]
        for ii, f in enumerate(seq):
            if ii == len(seq) - 1:
                p.op("vector", f, sm["ev"])
            else:
                p.op("vector", f)
                p.drain("vector")

    def kv_pipeline(kvh, prefix, koff, xts):
        ntok = koff + T
        lt = load_wstrip(wst, d["attn_Wk"], kvh * 128)

        def ev_k(pb_, c, rel0, n):
            tgt = sm["mm"].n
            p.wait("scalar", sm["mm"], tgt)
            p.op("scalar", lambda e, pb2=pb_, r2=rel0, n2=n:
                 e.copy(kTh[:, r2:r2 + n2], pb2[:, 0:n2]), sm["sc"])
        proj_strip(wst, xts, 0, ntok, ev_k, lt)
        if not prefix and kvh == KVH - 1:
            p.wait("sync", sm["sc"])
            dbg("kTh_early", kTh[:], [128, TH], BF16)
        lt = load_wstrip(wst2, d["attn_Wv"], kvh * 128)
        p.wait("tensor", sm["sc"])

        def ev_v(pb_, c, rel0, n):
            tgt = sm["mm"].n
            p.wait("scalar", sm["mm"], tgt)
            p.op("scalar", lambda e, pb2=pb_, r2=rel0, n2=n:
                 e.copy(vTst[:, r2:r2 + n2], pb2[:, 0:n2]), sm["sc"])
        proj_strip(wst2, xts, 0, ntok, ev_v, lt)
        p.wait("sync", sm["sc"])
        src3 = vTst[:, koff:koff + NT * 128].rearrange("p (g f) -> p g f", f=128)
        dst3 = v_gla[:].rearrange("p (g f) -> p g f", f=128)
        tg_vg = p.dmat("sync", dst3, src3, sm["tr"])
        if not prefix:
            src3 = vTst[:, 0:(NT + 1) * 128].rearrange("p (g f) -> p g f", f=128)
            dst3 = v_nt[:].rearrange("p (g f) -> p g f", f=128)
            tg_vg = p.dmat("sync", dst3, src3, sm["tr"])
            p.wait("vector", sm["sc"])
            for cc in range(0, ntok, 512):
                nn = min(512, ntok - cc)
                rope_emit(skh, kTh, cc, nn, cc)
        p.wait("tensor", sm["sc"])
        for tt in range(NT):
            p.op("tensor", lambda e, t2=tt: e.transpose(
                pF[:, 0:128], kTh[:, koff + t2 * 128: koff + (t2 + 1) * 128], cst["ident_b"][:]), sm["mm"])
            p.wait("scalar", sm["mm"])
            p.op("scalar", lambda e, t2=tt: e.copy(gs_h[:, t2 * 128:(t2 + 1) * 128], pF[:, 0:128]), sm["sc"])
            p.wait("tensor", sm["sc"])
        p.drain("scalar")
        p.wait("vector", sm["sc"])
        for tt in range(NT):
            sl = slice(tt * 128, (tt + 1) * 128)
            p.op("vector", lambda e, s=sl: e.reduce_max(red[:, 0:1], gs_h[:, s], axis=AX.X, negate=True), sm["ev"])
            p.wait("scalar", sm["ev"])
            p.op("scalar", lambda e, s=sl: e.activation(
                scr[:, 0:128], gs_h[:, s], AF.Exp, bias=red[:, 0:1], accum_out=red[:, 1:2]), sm["sc"])
            p.wait("vector", sm["sc"])
            p.op("vector", lambda e: e.reciprocal(red[:, 2:3], red[:, 1:2]))
            p.drain("vector")
            p.op("vector", lambda e, s=sl: e.tensor_scalar(
                ks_h[:, s], scr[:, 0:128], red[:, 2:3], None, ALU.mult), sm["ev"])
        p.wait("scalar", sm["ev"])
        p.op("scalar", lambda e: e.activation(scr[:, 0:1024], gs_h[:, 0:1024], AF.Sigmoid), sm["sc"])
        p.drain("scalar")
        p.op("scalar", lambda e: e.activation(gs_h[:, 0:1024], scr[:, 0:1024], AF.Ln), sm["sc"])
        if not prefix and kvh == KVH - 1:
            p.wait("sync", sm["sc"])
            p.wait("sync", sm["ev"])
            dbg("kTh_mid", kTh[:], [128, TH], BF16)
        p.wait("tensor", sm["ev"])
        p.wait("tensor", sm["sc"])
        p.wait("tensor", sm["tr"], tg_vg)
        Scur = S_f[:, kvh * 128:(kvh + 1) * 128]
        if not prefix:
            p.op("vector", lambda e: e.tensor_copy(S_snap[:, 0:128], Scur), sm["ev"])
        for tt in range(NT):
            gsl = gs_h[:, tt * 128:(tt + 1) * 128]
            ksl = ks_h[:, tt * 128:(tt + 1) * 128]
            p.op("tensor", lambda e, g=gsl: e.matmul(pC[:, 0:128], cst["tri_rev"][:], g, start=True, stop=True))
            p.op("tensor", lambda e, g=gsl: e.matmul(pD[:, 0:128], g, cst["tri_incl"][:], start=True, stop=True))
            p.op("tensor", lambda e, g=gsl: e.matmul(pC[:, 128:256], cst["tri_incl"][:], g, start=True, stop=True), sm["mm"])
            mm0 = sm["mm"].n
            p.wait("scalar", sm["mm"], mm0)
            p.op("scalar", lambda e: e.activation(scr[:, 0:128], pC[:, 0:128], AF.Exp))
            p.drain("scalar")
            p.op("scalar", lambda e, t2=tt: e.activation(ea_all[:, t2 * 128:(t2 + 1) * 128], pC[:, 128:256], AF.Exp))
            p.drain("scalar")
            p.op("scalar", lambda e: e.activation(scr[:, 128:256], pC[:, 128:256], AF.Exp, scale=-1.0))
            p.drain("scalar")
            p.op("scalar", lambda e: e.activation(red[:, 4:5], pD[:, 127:128], AF.Exp), sm["sc"])
            sc0 = sm["sc"].n
            p.wait("vector", sm["sc"], sc0)
            p.op("vector", lambda e, k=ksl: e.tensor_mul(scrb[:, 0:128], k, scr[:, 0:128]))
            p.drain("vector")
            p.op("vector", lambda e, k=ksl: e.tensor_mul(scrb[:, 128:256], k, scr[:, 128:256]), sm["ev"])
            ev0 = sm["ev"].n
            p.wait("tensor", sm["ev"], ev0)
            p.op("tensor", lambda e: e.transpose(pF[:, 0:128], scrb[:, 128:256], cst["ident_b"][:]), sm["mm"])
            p.wait("vector", sm["mm"])
            p.op("vector", lambda e, t2=tt: e.tensor_copy(kt_all[:, t2 * 128:(t2 + 1) * 128], pF[:, 0:128]), sm["ev"])
            p.op("tensor", lambda e, t2=tt: e.matmul(
                pE[:, 0:128], scrb[:, 0:128], v_gla[:, t2 * 128:(t2 + 1) * 128], start=True, stop=True), sm["mm"])
            mm1 = sm["mm"].n
            p.wait("vector", sm["mm"], mm1)
            p.op("vector", lambda e, S=Scur: e.scalar_tensor_tensor(
                S, S, red[:, 4:5], pE[:, 0:128], ALU.mult, ALU.add))
            p.drain("vector")
            if not prefix:
                p.op("vector", lambda e, t2=tt, S=Scur: e.tensor_copy(
                    S_snap[:, (t2 + 1) * 128:(t2 + 2) * 128], S), sm["ev"])
            else:
                p.op("vector", lambda e: e.engine_nop() if False else e.tensor_copy(red[:, 8:9], red[:, 4:5]), sm["ev"])
            evS = sm["ev"].n
            p.wait("tensor", sm["ev"], evS)
        if prefix:
            return
        for qh in range(GRP):
            hq = kvh * GRP + qh
            lt = load_wstrip(wst, d["attn_Wq"], hq * 128)

            def ev_q(pb_, c, rel0, n):
                tgt = sm["mm"].n
                p.wait("scalar", sm["mm"], tgt)
                p.op("scalar", lambda e, pb2=pb_, r2=rel0, n2=n:
                     e.copy(qTh[:, r2:r2 + n2], pb2[:, 0:n2]), sm["sc"])
            proj_strip(wst, xts, HALO, T, ev_q, lt)
            p.wait("vector", sm["sc"])
            for cc in range(0, T, 512):
                nn = min(512, T - cc)
                rope_emit(sqh, qTh, cc, nn, HALO + cc)
            p.wait("tensor", sm["ev"])
            for tt in range(NT):
                p.op("tensor", lambda e, t2=tt: e.transpose(
                    pF[:, 0:128], qTh[:, t2 * 128:(t2 + 1) * 128], cst["ident_b"][:]), sm["mm"])
                mmq = sm["mm"].n
                p.wait("vector", sm["mm"], mmq)
                p.op("vector", lambda e: e.reduce_max(red[:, 0:1], pF[:, 0:128], axis=AX.X, negate=True), sm["ev"])
                p.wait("scalar", sm["ev"])
                p.op("scalar", lambda e: e.activation(
                    scr[:, 0:128], pF[:, 0:128], AF.Exp, bias=red[:, 0:1], accum_out=red[:, 1:2]), sm["sc"])
                p.wait("vector", sm["sc"])
                p.op("vector", lambda e: e.reciprocal(red[:, 2:3], red[:, 1:2]))
                p.drain("vector")
                p.op("vector", lambda e: e.tensor_scalar(
                    scr[:, 128:256], scr[:, 0:128], red[:, 2:3], 0.5, ALU.mult, ALU.mult))
                p.drain("vector")
                p.op("vector", lambda e, t2=tt: e.tensor_mul(
                    scrb[:, 0:128], scr[:, 128:256], ea_all[:, t2 * 128:(t2 + 1) * 128]), sm["ev"])
                ev1 = sm["ev"].n
                p.wait("tensor", sm["ev"], ev1)
                p.op("tensor", lambda e: e.transpose(pH[:, 0:128], scrb[:, 0:128], cst["ident_b"][:]), sm["mm"])
                p.wait("vector", sm["mm"])
                p.op("vector", lambda e: e.tensor_copy(scrb[:, 256:384], pH[:, 0:128]), sm["ev"])
                ev2 = sm["ev"].n
                p.wait("tensor", sm["ev"], ev2)
                p.op("tensor", lambda e, t2=tt: e.matmul(
                    pE[:, 0:128], kt_all[:, t2 * 128:(t2 + 1) * 128], scrb[:, 256:384],
                    start=True, stop=True), sm["mm"])
                p.wait("vector", sm["mm"])
                p.op("vector", lambda e: e.tensor_mul(scrb[:, 384:512], pE[:, 0:128], cst["mask_tri"][:]), sm["ev"])
                ev3 = sm["ev"].n
                p.op("tensor", lambda e, t2=tt: e.matmul(
                    pA[:, 0:192], sqh[:, t2 * 128:(t2 + 1) * 128], skh[:, t2 * 128:t2 * 128 + 192],
                    start=True, stop=True), sm["mm"])
                mmS = sm["mm"].n
                p.wait("vector", sm["mm"], mmS)
                mtile = cst["mask_first"] if tt == 0 else cst["mask_rest"]
                p.op("vector", lambda e, m=mtile: e.scalar_tensor_tensor(
                    scr[:, 256:448], pA[:, 0:192], ISQ, m[:], ALU.mult, ALU.add))
                p.drain("vector")
                p.op("vector", lambda e: e.reduce_max(red[:, 0:1], scr[:, 256:448], axis=AX.X, negate=True), sm["ev"])
                p.wait("scalar", sm["ev"])
                p.op("scalar", lambda e: e.activation(
                    scr[:, 448:640], scr[:, 256:448], AF.Exp, bias=red[:, 0:1], accum_out=red[:, 1:2]), sm["sc"])
                p.wait("vector", sm["sc"])
                p.op("vector", lambda e: e.reciprocal(red[:, 2:3], red[:, 1:2]))
                p.drain("vector")
                p.op("vector", lambda e: e.tensor_scalar(
                    scrb[:, 512:704], scr[:, 448:640], red[:, 2:3], 0.5, ALU.mult, ALU.mult), sm["ev"])
                ev4 = sm["ev"].n
                p.wait("tensor", sm["ev"], ev4)
                p.op("tensor", lambda e: e.transpose(pH[:, 128:256], scrb[:, 512:640], cst["ident_b"][:]))
                p.op("tensor", lambda e: e.transpose(pH[0:64, 256:384], scrb[:, 640:704], cst["ident_b"][:]), sm["mm"])
                mmT = sm["mm"].n
                p.wait("vector", sm["mm"], mmT)
                p.op("vector", lambda e: e.tensor_copy(scrb[:, 704:832], pH[:, 128:256]))
                p.drain("vector")
                p.op("vector", lambda e: e.tensor_copy(scrb[0:64, 832:960], pH[0:64, 256:384]), sm["ev"])
                ev5 = sm["ev"].n
                p.wait("tensor", sm["ev"], ev5)
                p.op("tensor", lambda e, t2=tt: e.matmul(
                    pG[:, 0:128], S_snap[:, t2 * 128:(t2 + 1) * 128], scrb[:, 256:384],
                    start=True, stop=False))
                p.op("tensor", lambda e, t2=tt: e.matmul(
                    pG[:, 0:128], v_gla[:, t2 * 128:(t2 + 1) * 128], scrb[:, 384:512],
                    start=False, stop=False))
                p.op("tensor", lambda e, t2=tt: e.matmul(
                    pG[:, 0:128], v_nt[:, t2 * 128:(t2 + 1) * 128], scrb[:, 704:832],
                    start=False, stop=False))
                p.op("tensor", lambda e, t2=tt: e.matmul(
                    pG[:, 0:128], v_nt[0:64, (t2 + 1) * 128:(t2 + 2) * 128], scrb[0:64, 832:960],
                    start=False, stop=True), sm["mm"])
                mmO = sm["mm"].n
                p.wait("scalar", sm["mm"], mmO)
                p.op("scalar", lambda e, t2=tt, h2=hq: e.copy(
                    o_T[:, h2 * T + t2 * 128: h2 * T + (t2 + 1) * 128], pG[:, 0:128]), sm["sc"])
                p.wait("tensor", sm["sc"])

    for pb in range(PFB):
        p.op("vector", lambda e: e.memset(ar2[:, 0:DT * 1024], 0.0), sm["ev"])
        p.wait("sync", sm["ev"])
        p.wait("gpsimd", sm["ev"])
        build_xT(ar2, 1024, d["x_prefix"][pb * T:(pb + 1) * T, :], T, f"xp{pb}")
        barrier()
        for kvh in range(KVH):
            kv_pipeline(kvh, prefix=True, koff=0, xts=1024)
            barrier()

    p.op("vector", lambda e: e.memset(ar2[:], 0.0), sm["ev"])
    p.wait("sync", sm["ev"])
    p.wait("gpsimd", sm["ev"])
    build_xT(ar2, XTS, d["x_main"], TH, "xmain")
    barrier()
    dbg("xT", ar2[:], [128, DT * (TH + HALO)], BF16)

    for kvh in range(KVH):
        kv_pipeline(kvh, prefix=False, koff=HALO, xts=XTS)
        barrier()
    dbg("o_T", o_T[:], [128, H * T], BF16)
    dbg("S_f", S_f[:], [128, KVH * 128], F32)
    dbg("kTh", kTh[:], [128, TH], BF16)
    dbg("skh", skh[:], [128, TH], BF16)
    dbg("ks_h", ks_h[:], [128, T], BF16)
    dbg("gs_h", gs_h[:], [128, T], F32)
    dbg("ea_all", ea_all[:], [128, NT * 128], F32)
    dbg("kt_all", kt_all[:], [128, NT * 128], BF16)
    dbg("S_snap", S_snap[:], [128, (NT + 1) * 128], BF16)
    dbg("sqh", sqh[:], [128, T], BF16)
    dbg("qTh", qTh[:], [128, T], BF16)
    dbg("v_nt", v_nt[:], [128, (NT + 1) * 128], BF16)
    dbg("v_gla", v_gla[:], [128, NT * 128], BF16)


    if stop_after < 5:
        meta = dict(nc=nc, ctx=ctx, prog=p, sems=sm, dram=d, sbuf=sb_, cst=cst)
        return meta
    ar2f = ar2[:].bitcast(F32)
    Y1S = 4096

    def ln_tile(buf_ap, ncols, out_cb):
        p.op("vector", lambda e: e.reduce_sum(red[:, 0:1], buf_ap, axis=AX.X))
        p.drain("vector")
        p.op("vector", lambda e: e.tensor_scalar(red[:, 0:1], red[:, 0:1], 1.0 / ncols, None, ALU.mult), sm["ev"])
        p.wait("scalar", sm["ev"])
        wf0, wf1 = wst[:].bitcast(F32), wst2[:].bitcast(F32)
        half = ncols // 2
        p.wait("scalar", sm["tr"])
        p.op("scalar", lambda e: e.activation(wf0[:, 0:half], buf_ap[:, 0:half], AF.Square,
                                              accum_out=red[:, 1:2]))
        p.drain("scalar")
        p.op("scalar", lambda e: e.activation(wf1[:, 0:half], buf_ap[:, half:ncols], AF.Square,
                                              accum_out=red[:, 4:5]), sm["sc"])
        p.wait("vector", sm["sc"])
        p.op("vector", lambda e: e.tensor_add(red[:, 1:2], red[:, 1:2], red[:, 4:5]))
        p.drain("vector")
        p.op("vector", lambda e: e.tensor_scalar(red[:, 1:2], red[:, 1:2], 1.0 / ncols, None, ALU.mult))
        p.drain("vector")
        p.op("vector", lambda e: e.tensor_mul(red[:, 4:5], red[:, 0:1], red[:, 0:1]))
        p.drain("vector")
        p.op("vector", lambda e: e.tensor_sub(red[:, 1:2], red[:, 1:2], red[:, 4:5]), sm["ev"])
        p.wait("scalar", sm["ev"])
        p.op("scalar", lambda e: e.activation(red[:, 2:3], red[:, 1:2], AF.Ln, bias=red[:, 9:10]))
        p.drain("scalar")
        p.op("scalar", lambda e: e.activation(red[:, 2:3], red[:, 2:3], AF.Exp, scale=-0.5), sm["sc"])
        p.wait("vector", sm["sc"])
        p.op("vector", lambda e: e.tensor_scalar(buf_ap, buf_ap, red[:, 0:1], red[:, 2:3],
                                                 ALU.subtract, ALU.mult), sm["ev"])
        p.drain("vector")
        out_cb()

    for tc in range(TC):
        p.wait("vector", sm["st"])
        for jt in range(DT):
            wbuf = wst if jt % 2 == 0 else wst2
            lt = load_wstrip(wbuf, d["attn_Wo"], jt * 128)
            p.wait("tensor", sm["lw1"] if jt % 2 == 0 else sm["lw2"], lt)
            pb_ = pA if jt % 2 == 0 else pB
            for ot in range(DT):
                lhs_ap = wbuf[:, ot * 128:(ot + 1) * 128]
                rhs_ap = o_T[:, ot * T + tc * 512: ot * T + tc * 512 + 512]
                p.op("tensor", lambda e, pb2=pb_, L=lhs_ap, R=rhs_ap,
                     s=(ot == 0), st=(ot == DT - 1):
                     e.matmul(pb2[:, 0:512], L, R, start=s, stop=st),
                     sem=sm["mm"] if ot == DT - 1 else None)
            p.wait("vector", sm["mm"])
            slot = (jt % 2) * 512
            p.op("vector", lambda e, pb2=pb_, s0=slot: e.tensor_copy(scr[:, s0:s0 + 512], pb2[:, 0:512]), sm["ev"])
            ev0 = sm["ev"].n
            p.wait("tensor", sm["ev"], ev0)
            for q4 in range(4):
                p.op("tensor", lambda e, s0=slot, q2=q4: e.transpose(
                    pE[:, q2 * 128:(q2 + 1) * 128], scr[:, s0 + q2 * 128: s0 + (q2 + 1) * 128],
                    cst["ident_f"][:]), sem=sm["mm"] if q4 == 3 else None)
            mmT = sm["mm"].n
            p.wait("vector", sm["mm"], mmT)
            for q4 in range(4):
                dst = ar2f[:, q4 * Y1S + jt * 128: q4 * Y1S + (jt + 1) * 128]
                p.op("vector", lambda e, dd=dst, q2=q4: e.tensor_copy(dd, pE[:, q2 * 128:(q2 + 1) * 128]),
                     sem=sm["ev"] if q4 == 3 else None)
            evT = sm["ev"].n
            p.wait("tensor", sm["ev"], evT)
        p.wait("gpsimd", sm["ev"])
        p.wait("gpsimd", sm["mm"])
        for q4 in range(4):
            tglob = tc * 512 + q4 * 128
            wfs = [wst[:].bitcast(F32), wst2[:].bitcast(F32)]
            p.dma("gpsimd", wfs[0][:, 0:2048],
                  d["x_main"][HALO + tglob: HALO + tglob + 128, 0:2048], sm["lw1"])
            p.dma("gpsimd", wfs[1][:, 0:2048],
                  d["x_main"][HALO + tglob: HALO + tglob + 128, 2048:4096], sm["lw2"])
            p.wait("vector", sm["lw1"])
            p.wait("vector", sm["lw2"])
            for hh in range(2):
                sl = ar2f[:, q4 * Y1S + hh * 2048: q4 * Y1S + (hh + 1) * 2048]
                p.op("vector", lambda e, s=sl, h2=hh: e.tensor_add(s, s, wfs[h2][:, 0:2048]),
                     sem=sm["ev"] if hh == 1 else None)
            p.drain("vector")
            p.wait("gpsimd", sm["ev"])

            def emit_y1(q2=q4, tg2=tglob):
                p.wait("sync", sm["ev"])
                p.dma("sync", d["sp_y1"][tg2:tg2 + 128, :], ar2f[:, q2 * Y1S:(q2 + 1) * Y1S], sm["st"])
                p.wait("gpsimd", sm["ev"])
                p.op("vector", lambda e: e.tensor_copy(wst[:, 0:4096], ar2f[:, q2 * Y1S:(q2 + 1) * Y1S]), sm["ev"])
                evc = sm["ev"].n
                p.wait("sync", sm["ev"], evc)
                src3 = wst[:, 0:4096].rearrange("p (dd f) -> p dd f", f=128)
                dst3 = o_T[:].rearrange("p (dd t) -> p dd t", t=T)[:, :, tg2:tg2 + 128]
                tgt = p.dmat("sync", dst3, src3, sm["tr"])
                p.wait("gpsimd", sm["tr"], tgt)
                p.wait("vector", sm["tr"], tgt)
            ln_tile(ar2f[:, q4 * Y1S:(q4 + 1) * Y1S], HID, emit_y1)
    barrier()
    dbg("y1T", o_T[:], [128, DT * T], BF16)


    if stop_after < 6:
        meta = dict(nc=nc, ctx=ctx, prog=p, sems=sm, dram=d, sbuf=sb_, cst=cst)
        return meta
    y1T = o_T
    p.wait("tensor", sm["tr"])
    p.wait("vector", sm["tr"])
    p.wait("scalar", sm["tr"])
    BSG, BSU, BSD = 22528, 28032, 33536

    def bs_ap(off, oi):
        pb0 = 64 * (oi % 2)
        c0 = off + (oi // 2) * 128
        return ar2[pb0:pb0 + 33, c0:c0 + 128]

    for tc in range(TC):
        for dd in range(DT):
            p.op("tensor", lambda e, d2=dd, c2=tc: e.matmul(
                pA[0:1, 0:512], cst["ones_b"][:], y1T[:, d2 * T + c2 * 512: d2 * T + c2 * 512 + 512],
                start=(d2 == 0), stop=(d2 == DT - 1)), sem=sm["mm"] if dd == DT - 1 else None)
        p.wait("scalar", sm["mm"])
        p.op("scalar", lambda e, c2=tc: e.copy(skh[0:1, c2 * 512: c2 * 512 + 512], pA[0:1, 0:512]), sm["sc"])
        p.wait("vector", sm["sc"])
        for rdst in (scrb, qTh):
            for rr in (32, 96):
                p.op("vector", lambda e, c2=tc, rd=rdst, r2=rr: e.tensor_copy(
                    rd[r2:r2 + 1, c2 * 512:c2 * 512 + 512], skh[0:1, c2 * 512:c2 * 512 + 512]))
                p.drain("vector")
        p.op("vector", lambda e: e.tensor_copy(red[:, 10:11], red[:, 9:10]), sm["ev"])
        p.wait("tensor", sm["sc"])

    for which, wdram, rows0 in (("g", d["ffn_gate_A"], 0), ("u", d["ffn_up_A"], 32)):
        p.wait("vector", sm["sc"])
        lt = load_wstrip(wst, wdram, 0, width=32)
        p.wait("tensor", sm["lw1"], lt)
        tbuf = scrb if which == "g" else qTh
        for tc in range(TC):
            for dd in range(DT):
                p.op("tensor", lambda e, d2=dd, c2=tc: e.matmul(
                    pA[0:32, 0:512], wst[:, d2 * 32:(d2 + 1) * 32],
                    y1T[:, d2 * T + c2 * 512: d2 * T + c2 * 512 + 512],
                    start=(d2 == 0), stop=(d2 == DT - 1)), sem=sm["mm"] if dd == DT - 1 else None)
            p.wait("vector", sm["mm"])
            p.op("vector", lambda e, c2=tc, tb=tbuf: e.tensor_scalar(
                tb[0:32, c2 * 512:c2 * 512 + 512], pA[0:32, 0:512], 0.02, None, ALU.mult))
            p.drain("vector")
            p.op("vector", lambda e, c2=tc, tb=tbuf: e.tensor_scalar(
                tb[64:96, c2 * 512:c2 * 512 + 512], pA[0:32, 0:512], 0.02, None, ALU.mult), sm["ev"])
            p.wait("tensor", sm["ev"])

    def build_bs(bdram, sdram, n_tiles, dst_off):
        for oi in range(n_tiles):
            tg = p.dma("sync", scr[0:32, 0:128], bdram[0:32, oi * 128:(oi + 1) * 128], sm["ldh"])
            tg2 = p.dma("sync", red[:, 5:6],
                        sdram[oi * 128:(oi + 1) * 128].rearrange("(p o) -> p o", o=1), sm["ldh"])
            p.wait("vector", sm["ldh"], tg2)
            p.op("vector", lambda e: e.tensor_scalar(red[:, 6:7], red[:, 5:6], 0.005, None, ALU.mult))
            p.drain("vector")
            p.op("vector", lambda e: e.reciprocal(red[:, 7:8], red[:, 6:7]), sm["ev"])
            p.drain("vector")
            ev0 = sm["ev"].n
            p.wait("tensor", sm["ldh"])
            p.op("tensor", lambda e: e.transpose(pE[:, 0:32], scr[0:32, 0:128], cst["ident_f"][0:32, 0:32]), sm["mm"])
            mm0 = sm["mm"].n
            p.wait("vector", sm["mm"], mm0)
            p.op("vector", lambda e: e.tensor_scalar(vTst[:, 0:32], pE[:, 0:32], red[:, 7:8], None, ALU.mult), sm["ev"])
            ev1 = sm["ev"].n
            p.wait("tensor", sm["ev"], ev1)
            p.op("tensor", lambda e: e.transpose(pF[0:32, 0:128], vTst[:, 0:32], cst["ident_b"][:]), sm["mm"])
            mm1 = sm["mm"].n
            p.wait("scalar", sm["mm"], mm1)
            p.op("scalar", lambda e, o2=oi: e.copy(
                ar2[64 * (o2 % 2):64 * (o2 % 2) + 32, dst_off + (o2 // 2) * 128: dst_off + (o2 // 2) * 128 + 128],
                pF[0:32, 0:128]), sm["sc"])
            p.wait("tensor", sm["sc"])
            p.wait("sync", sm["sc"])
    p.wait("vector", sm["st"])
    p.wait("scalar", sm["st"])
    p.op("vector", lambda e: e.memset(ar2[32:33, BSG:BSG + 13056], -7.5))
    p.drain("vector")
    p.op("vector", lambda e: e.memset(ar2[96:97, BSG:BSG + 13056], -7.5), sm["ev"])
    p.wait("sync", sm["ev"])
    p.wait("scalar", sm["ev"])
    build_bs(d["ffn_gate_B"], d["ffn_gate_scale"], IT, BSG)
    build_bs(d["ffn_up_B"], d["ffn_up_scale"], IT, BSU)
    build_bs(d["ffn_down_B"], d["ffn_down_scale"], DT, BSD)
    p.op("vector", lambda e: e.memset(kTh[:, 32:33], 1.0), sm["ev"])
    barrier()

    if stop_after < 6.5:
        meta = dict(nc=nc, ctx=ctx, prog=p, sems=sm, dram=d, sbuf=sb_, cst=cst)
        return meta
    QW = [22, 22, 21, 21]
    QB = [0, 22, 44, 65]
    t3_open = [False]

    def ffn_quarter(qt, parts="ab", maxoi=99, skips=""):
        qw, qb = QW[qt], QB[qt]
        p.wait("vector", sm["st"])
        p.wait("scalar", sm["st"])
        for ol in range(min(qw, maxoi) if "a" in parts else 0):
            oi = qb + ol
            p.wait("gpsimd", sm["mm"])
            ltg = p.dma("gpsimd", wst[:, 0:4096], d["ffn_gate_q"][oi * 128:(oi + 1) * 128, :], sm["lw1"])
            p.wait("sync", sm["lw1"], ltg)
            p.wait("sync", sm["mm"])
            src3 = wst[:, 0:4096].rearrange("p (dd f) -> p dd f", f=128)
            dst3 = wst2[:, 0:4096].rearrange("p (dd f) -> p dd f", f=128)
            tgt = p.dmat("sync", dst3, src3, sm["tr"])
            p.wait("sync", sm["ev"])
            tgs = p.dma("sync", red[:, 5:6],
                        d["ffn_gate_scale"][oi * 128:(oi + 1) * 128].rearrange("(p o) -> p o", o=1), sm["ldh"])
            tgs2 = p.dma("sync", red[:, 6:7],
                         d["ffn_up_scale"][oi * 128:(oi + 1) * 128].rearrange("(p o) -> p o", o=1), sm["ldh"])
            p.wait("vector", sm["ldh"], tgs2)
            p.op("vector", lambda e: e.tensor_scalar(red[:, 5:6], red[:, 5:6], 0.005, None, ALU.mult))
            p.drain("vector")
            p.op("vector", lambda e: e.tensor_scalar(red[:, 6:7], red[:, 6:7], 0.005, None, ALU.mult), sm["ev"])
            tga = p.dma("gpsimd", kTh[:, 0:32], d["ffn_down_A"][oi * 128:(oi + 1) * 128, :], sm["lda"])
            p.wait("tensor", sm["tr"], tgt)
            for tc in range(TC):
                pb_ = pA if tc == 0 else pB
                pr0 = 64 * (oi % 2)
                p.op("tensor", lambda e, pb2=pb_, o2=oi, c2=tc, r0=pr0: e.matmul(
                    pb2[:, 0:512], bs_ap(BSG, o2),
                    scrb[r0:r0 + 33, c2 * 512:c2 * 512 + 512], start=True, stop=False))
                for dd in range(DT):
                    p.op("tensor", lambda e, pb2=pb_, d2=dd, c2=tc: e.matmul(
                        pb2[:, 0:512], wst2[:, d2 * 128:(d2 + 1) * 128],
                        y1T[:, d2 * T + c2 * 512: d2 * T + c2 * 512 + 512],
                        start=False, stop=(d2 == DT - 1)),
                        sem=sm["mm"] if dd == DT - 1 else None)
            mmg = sm["mm"].n
            p.wait("gpsimd", sm["tr"], tgt)
            ltu = p.dma("gpsimd", wst[:, 0:4096], d["ffn_up_q"][oi * 128:(oi + 1) * 128, :], sm["lw1"])
            p.wait("sync", sm["lw1"], ltu)
            p.wait("sync", sm["mm"], mmg)
            tgt2 = p.dmat("sync", dst3, src3, sm["tr"])
            p.wait("tensor", sm["tr"], tgt2)
            for tc in range(TC):
                pb_ = pC if tc == 0 else pD
                pr0 = 64 * (oi % 2)
                p.op("tensor", lambda e, pb2=pb_, o2=oi, c2=tc, r0=pr0: e.matmul(
                    pb2[:, 0:512], bs_ap(BSU, o2),
                    qTh[r0:r0 + 33, c2 * 512:c2 * 512 + 512], start=True, stop=False))
                for dd in range(DT):
                    p.op("tensor", lambda e, pb2=pb_, d2=dd, c2=tc: e.matmul(
                        pb2[:, 0:512], wst2[:, d2 * 128:(d2 + 1) * 128],
                        y1T[:, d2 * T + c2 * 512: d2 * T + c2 * 512 + 512],
                        start=False, stop=(d2 == DT - 1)),
                        sem=sm["mm"] if dd == DT - 1 else None)
            mmu = sm["mm"].n
            for tc in range(TC):
                pg_ = pA if tc == 0 else pB
                pu_ = pC if tc == 0 else pD
                p.wait("vector", sm["mm"], mmu)
                p.wait("vector", sm["ev"])
                p.op("vector", lambda e, pg2=pg_: e.tensor_scalar(
                    scr[:, 0:512], pg2[:, 0:512], red[:, 5:6], None, ALU.mult), sm["ev"])
                p.wait("scalar", sm["ev"])
                p.op("scalar", lambda e: e.activation(scr[:, 512:1024], scr[:, 0:512], AF.Silu), sm["sc"])
                p.op("vector", lambda e, pu2=pu_: e.tensor_scalar(
                    scr[:, 1024:1536], pu2[:, 0:512], red[:, 6:7], None, ALU.mult))
                p.drain("vector")
                p.wait("vector", sm["sc"])
                p.op("vector", lambda e, o2=ol, c2=tc: e.tensor_mul(
                    ar2[:, o2 * T + c2 * 512: o2 * T + c2 * 512 + 512],
                    scr[:, 512:1024], scr[:, 1024:1536]), sm["ev"])
            evh = sm["ev"].n
            p.wait("tensor", sm["ev"], evh)
            p.wait("tensor", sm["lda"])
            for tc in range(TC if "t" not in skips else 0):
                pt_ = pE if tc == 0 else pG
                p.op("tensor", lambda e, pt2=pt_, o2=ol, c2=tc: e.matmul(
                    pt2[0:33, 0:512], kTh[:, 0:33],
                    ar2[:, o2 * T + c2 * 512: o2 * T + c2 * 512 + 512],
                    start=True, stop=True), sem=sm["mm"])
                mmt = sm["mm"].n
                p.wait("vector", sm["mm"], mmt)
                if qt == 0 and ol == 0:
                    p.op("vector", lambda e, pt2=pt_, c2=tc: e.tensor_copy(
                        ea_all[0:33, c2 * 512:c2 * 512 + 512], pt2[0:33, 0:512]), sm["ev"])
                else:
                    p.op("vector", lambda e, pt2=pt_, c2=tc: e.tensor_add(
                        ea_all[0:33, c2 * 512:c2 * 512 + 512],
                        ea_all[0:33, c2 * 512:c2 * 512 + 512], pt2[0:33, 0:512]), sm["ev"])
                evt = sm["ev"].n
                p.wait("tensor", sm["ev"], evt)
        p.wait("sync", sm["ev"])
        dbg(f"hm_q{qt}", ar2[:, 0:qw * T], [128, qw * T], BF16)
        if qt == 3:
            for tc in range(TC):
                sl = slice(tc * 512, tc * 512 + 512)
                p.op("vector", lambda e, s=sl: e.tensor_scalar(
                    sqh[0:32, s], ea_all[0:32, s], 0.02, None, ALU.mult))
                p.drain("vector")
                p.op("vector", lambda e, s=sl: e.tensor_scalar(
                    sqh[64:96, s], ea_all[0:32, s], 0.02, None, ALU.mult))
                p.drain("vector")
                p.op("vector", lambda e, s=sl: e.tensor_copy(sqh[32:33, s], ea_all[32:33, s]))
                p.drain("vector")
                p.op("vector", lambda e, s=sl: e.tensor_copy(sqh[96:97, s], ea_all[32:33, s]), sm["ev"])
            p.wait("tensor", sm["ev"])
            p.wait("sync", sm["ev"])
            dbg("t3T", sqh[:], [128, T], BF16)
            dbg("rrow", skh[:], [128, TH], BF16)
            dbg("t1g", scrb[:], [128, 1024], BF16)
            dbg("t1u", qTh[:], [128, T], BF16)
        iw = qw * 128
        for jglob in range(DT if "b" in parts else 0):
            j0 = jglob * 128
            p.wait("gpsimd", sm["mm"])
            ltd = p.dma("gpsimd", wst[:, 0:iw],
                        d["ffn_down_q"][j0:j0 + 128, qb * 128: qb * 128 + iw], sm["lw1"])
            p.wait("sync", sm["lw1"], ltd)
            p.wait("sync", sm["mm"])
            src3 = wst[:, 0:iw].rearrange("p (it f) -> p it f", f=128)
            dst3 = wst2[:, 0:iw].rearrange("p (it f) -> p it f", f=128)
            tgt = p.dmat("sync", dst3, src3, sm["tr"])
            p.wait("tensor", sm["tr"], tgt)
            for tc in range(TC):
                pb_ = pA if tc == 0 else pB
                if qt == 3:
                    pr0 = 64 * (jglob % 2)
                    p.op("tensor", lambda e, pb2=pb_, jg2=jglob, c2=tc, r0=pr0: e.matmul(
                        pb2[:, 0:512], bs_ap(BSD, jg2),
                        sqh[r0:r0 + 33, c2 * 512:c2 * 512 + 512], start=True, stop=False))
                for ol in range(qw):
                    last = (ol == qw - 1)
                    p.op("tensor", lambda e, pb2=pb_, o2=ol, c2=tc, l2=last, s2=(ol == 0 and qt < 3): e.matmul(
                        pb2[:, 0:512], wst2[:, o2 * 128:(o2 + 1) * 128],
                        ar2[:, o2 * T + c2 * 512: o2 * T + c2 * 512 + 512],
                        start=s2, stop=l2),
                        sem=sm["mm"] if last else None)
            mmd = sm["mm"].n
            p.wait("vector", sm["mm"], mmd)
            for tc in range(TC):
                pb_ = pA if tc == 0 else pB
                if qt > 0:
                    tgl = p.dma("gpsimd", gs_h[:, tc * 512:tc * 512 + 512],
                                d["sp_yt"][j0:j0 + 128, tc * 512:tc * 512 + 512], sm["ldx"])
                    p.wait("vector", sm["ldx"], tgl)
                    p.op("vector", lambda e, pb2=pb_, c2=tc: e.tensor_add(
                        scr[:, 0:512], pb2[:, 0:512], gs_h[:, c2 * 512:c2 * 512 + 512]))
                    p.drain("vector")
                else:
                    p.op("vector", lambda e, pb2=pb_: e.tensor_copy(scr[:, 0:512], pb2[:, 0:512]))
                    p.drain("vector")
                if qt == 3:
                    tgs = p.dma("sync", red[:, 5:6],
                                d["ffn_down_scale"][j0:j0 + 128]
                                .rearrange("(p o) -> p o", o=1), sm["ldh"])
                    p.wait("vector", sm["ldh"], tgs)
                    p.op("vector", lambda e: e.tensor_scalar(red[:, 5:6], red[:, 5:6], 0.005, None, ALU.mult))
                    p.drain("vector")
                    p.op("vector", lambda e: e.tensor_scalar(scr[:, 0:512], scr[:, 0:512], red[:, 5:6], None, ALU.mult))
                    p.drain("vector")
                p.wait("vector", sm["st"])
                p.op("vector", lambda e, c2=tc: e.tensor_copy(
                    scr[:, 512 + c2 * 512:1024 + c2 * 512], scr[:, 0:512]), sm["ev"])
                p.drain("vector")
                evd = sm["ev"].n
                p.wait("sync", sm["ev"], evd)
                p.dma("sync", d["sp_yt"][j0:j0 + 128, tc * 512:tc * 512 + 512],
                      scr[:, 512 + tc * 512:1024 + tc * 512], sm["st"])
                p.wait("tensor", sm["ev"], evd)
            p.wait("gpsimd", sm["st"])

    _parts = globals().get("FFN_PARTS", "ab")
    _maxoi = globals().get("FFN_MAXOI", 99)
    _sk = globals().get("FFN_SKIP", "")
    for qt in range(4):
        if qt >= stop_after - 6:
            break
        ffn_quarter(qt, _parts, _maxoi, _sk)
        barrier()

    if stop_after < 11:
        meta = dict(nc=nc, ctx=ctx, prog=p, sems=sm, dram=d, sbuf=sb_, cst=cst)
        return meta
    for tt in range(NT):
        p.wait("sync", sm["st"])
        p.wait("gpsimd", sm["st"])
        tgh = p.dma("sync", ar2f[:, 0:4096], d["sp_y1"][tt * 128:(tt + 1) * 128, :], sm["ldh"])
        for rr in range(4):
            tg = p.dma("gpsimd",
                       ar2f[:, 4096 + rr * 1024: 4096 + (rr + 1) * 1024].rearrange("p (dt f) -> p dt f", f=128),
                       d["sp_yt"][rr * 1024:(rr + 1) * 1024, tt * 128:(tt + 1) * 128]
                       .rearrange("(dt p) f -> p dt f", p=128), sm["ldx"])
        p.wait("tensor", sm["ldx"], tg)
        p.wait("tensor", sm["ldh"], tgh)
        p.wait("tensor", sm["ev"])
        for q8 in range(8):
            for q4 in range(4):
                dtile = q8 * 4 + q4
                p.op("tensor", lambda e, d2=dtile, q2=q4: e.transpose(
                    pE[:, q2 * 128:(q2 + 1) * 128], ar2f[:, 4096 + d2 * 128: 4096 + (d2 + 1) * 128],
                    cst["ident_f"][:]), sem=sm["mm"] if q4 == 3 else None)
            mmq = sm["mm"].n
            p.wait("vector", sm["mm"], mmq)
            sl = ar2f[:, q8 * 512:(q8 + 1) * 512]
            p.op("vector", lambda e, s=sl: e.tensor_add(s, s, pE[:, 0:512]), sm["ev"])
            evq = sm["ev"].n
            p.wait("tensor", sm["ev"], evq)

        def emit_out(tt2=tt):
            p.wait("sync", sm["ev"])
            p.dma("sync", d["out"][tt2 * 128:(tt2 + 1) * 128, :], ar2f[:, 0:4096], sm["st"])
        ln_tile(ar2f[:, 0:4096], HID, emit_out)
        p.wait("gpsimd", sm["ev"])
    p.wait("sync", sm["st"])

    meta = dict(nc=nc, ctx=ctx, prog=p, sems=sm, dram=d, sbuf=sb_, cst=cst)
    return meta


def finalize(meta):
    nc, ctx, p = meta["nc"], meta["ctx"], meta["prog"]
    blk = ctx.enter_context(nc.Block())
    p.run(blk)
    ctx.close()
    return nc


def kernel(**inputs):
    meta = build()
    nc = finalize(meta)
    maps = shard_inputs(inputs)
    res = run_bass_kernel_spmd(nc, maps, core_ids=list(range(NCORES)))
    out = np.zeros((B, N, HID), dtype=np.float32)
    for core in range(NCORES):
        b, j = divmod(core, 4)
        out[b, j * T:(j + 1) * T] = np.asarray(res.results[core]["out"], dtype=np.float32)
    return out


# version 03
# speedup vs baseline: 1.2222x; 1.2222x over previous
import numpy as np
import ml_dtypes
from contextlib import ExitStack

import concourse.bass as bass
import concourse.mybir as mybir
from concourse.bass_utils import run_bass_kernel_spmd

F32 = mybir.dt.float32
BF16 = mybir.dt.bfloat16
I32 = mybir.dt.int32
AF = mybir.ActivationFunctionType
ALU = mybir.AluOpType
AX = mybir.AxisListType
BF = ml_dtypes.bfloat16

B, N, HID = 2, 4096, 4096
H, KVH, Dh, W = 32, 8, 128, 64
GRP = H // KVH
I_FFN, R_LR = 11008, 32
NCORES = 8
T = 1024
TC = T // 512
NT = T // 128
HALO = 64
TH = T + HALO
PFB = 3
DT = HID // 128
IT = I_FFN // 128
NEG = -1.0e30
EPS = 1e-5

ENGINES = ("sync", "gpsimd", "scalar", "vector", "tensor")


class Sem:
    def __init__(self, nc, ctx, name):
        self.h = ctx.enter_context(nc.semaphore(name))
        self.n = 0

    def bump(self, k=1):
        self.n += k
        return self.n


class Prog:

    def __init__(self):
        self.q = {e: [] for e in ENGINES}

    def on(self, eng, fn):
        self.q[eng].append(fn)

    def dma(self, eng, out_ap, in_ap, sem, **kw):
        tgt = sem.bump(16)
        self.on(eng, lambda e, o=out_ap, i=in_ap, k=dict(kw): e.dma_start(out=o, in_=i, **k).then_inc(sem.h, 16))
        return tgt

    def dmat(self, eng, out_ap, in_ap, sem):
        tgt = sem.bump(16)
        self.on(eng, lambda e, o=out_ap, i=in_ap: e.dma_start_transpose(o, i).then_inc(sem.h, 16))
        return tgt

    def wait(self, eng, sem, val=None):
        v = sem.n if val is None else val
        if v > 0:
            self.on(eng, lambda e, s=sem.h, vv=v: e.wait_ge(s, vv))

    def op(self, eng, fn, sem=None, k=1):
        if sem is None:
            self.on(eng, fn)
            return None
        tgt = sem.bump(k)
        self.on(eng, lambda e, f=fn, s=sem.h, kk=k: f(e).then_inc(s, kk))
        return tgt

    def drain(self, eng):
        self.on(eng, lambda e: e.drain())

    def run(self, block):
        for name in ENGINES:
            sec = getattr(block, name)
            ops = self.q[name]

            def make(ops_):
                def body(e):
                    for f in ops_:
                        f(e)
                return body
            sec(make(ops))



def host_consts(core_id):
    b, j = divmod(core_id, 4)
    t0 = j * T
    c = {}
    c["ident_f"] = np.eye(128, dtype=np.float32)
    c["ident_b"] = np.eye(128, dtype=BF)
    jj, ii = np.meshgrid(np.arange(128), np.arange(128), indexing="ij")
    c["tri_incl"] = ((jj <= ii).astype(np.float32) / 16.0)
    c["tri_rev"] = ((jj > ii).astype(np.float32) / 16.0)
    c["mask_tri"] = (jj <= ii).astype(np.float32)
    r = np.arange(128)[:, None]
    col = np.arange(192)[None, :]
    maskA = np.where((col >= r) & (col <= r + 64), 0.0, NEG).astype(np.float32)
    maskB = np.where((col >= r) & (col <= r + 64) & (col >= 64), 0.0, NEG).astype(np.float32)
    c["mask_first"] = maskB if j == 0 else maskA
    c["mask_rest"] = maskA
    pos = np.arange(t0 - HALO, t0 + T, dtype=np.float64)
    half = np.arange(Dh // 2, dtype=np.float64)
    inv = 1.0 / (10000.0 ** (2.0 * half / Dh))
    fr = inv[:, None] * pos[None, :]
    emb = np.concatenate([fr, fr], axis=0)
    c["cos_t"] = np.cos(emb).astype(np.float32)
    c["sin_t"] = np.sin(emb).astype(np.float32)
    c["ones_b"] = np.ones((128, 1), dtype=BF)
    c["neg75"] = np.full((1, 128), -7.5, dtype=BF)
    return c


CONST_SPECS = [
    ("ident_f", [128, 128], F32), ("ident_b", [128, 128], BF16),
    ("tri_incl", [128, 128], F32), ("tri_rev", [128, 128], F32),
    ("mask_tri", [128, 128], F32), ("mask_first", [128, 192], F32),
    ("mask_rest", [128, 192], F32), ("cos_t", [128, TH], F32),
    ("sin_t", [128, TH], F32), ("ones_b", [128, 1], BF16),
    ("neg75", [1, 128], BF16),
]


def shard_inputs(inputs):
    x = inputs["L0_attn_hidden_states"]
    maps = []
    for core in range(NCORES):
        b, j = divmod(core, 4)
        t0 = j * T
        m = {}
        xm = np.zeros((TH, HID), dtype=np.float32)
        lo = max(t0 - HALO, 0)
        xm[HALO - (t0 - lo):] = x[b, lo:t0 + T]
        m["x_main"] = np.ascontiguousarray(xm)
        xp = np.zeros((PFB * T, HID), dtype=np.float32)
        if t0 > 0:
            xp[PFB * T - t0:] = x[b, 0:t0]
        m["x_prefix"] = xp
        for k, v in host_consts(core).items():
            m[k] = np.ascontiguousarray(v)
        for name in ("attn_Wq", "attn_Wk", "attn_Wv", "attn_Wo",
                     "ffn_gate_q", "ffn_gate_scale", "ffn_gate_A", "ffn_gate_B",
                     "ffn_up_q", "ffn_up_scale", "ffn_up_A", "ffn_up_B",
                     "ffn_down_q", "ffn_down_scale", "ffn_down_A", "ffn_down_B"):
            m[name] = inputs["L0_" + name]
        maps.append(m)
    return maps



def build(debug_taps=(), stop_after=99):
    nc = bass.Bass("TRN2", target_bir_lowering=False)
    ctx = ExitStack()
    d = {}
    d["x_main"] = nc.dram_tensor("x_main", [TH, HID], F32, kind="ExternalInput").ap()
    d["x_prefix"] = nc.dram_tensor("x_prefix", [PFB * T, HID], F32, kind="ExternalInput").ap()
    for name, shape, dt in CONST_SPECS:
        d[name] = nc.dram_tensor(name, shape, dt, kind="ExternalInput").ap()
    d["attn_Wq"] = nc.dram_tensor("attn_Wq", [HID, HID], F32, kind="ExternalInput").ap()
    d["attn_Wk"] = nc.dram_tensor("attn_Wk", [HID, KVH * Dh], F32, kind="ExternalInput").ap()
    d["attn_Wv"] = nc.dram_tensor("attn_Wv", [HID, KVH * Dh], F32, kind="ExternalInput").ap()
    d["attn_Wo"] = nc.dram_tensor("attn_Wo", [HID, HID], F32, kind="ExternalInput").ap()
    d["ffn_gate_q"] = nc.dram_tensor("ffn_gate_q", [I_FFN, HID], I32, kind="ExternalInput").ap()
    d["ffn_gate_scale"] = nc.dram_tensor("ffn_gate_scale", [I_FFN], F32, kind="ExternalInput").ap()
    d["ffn_gate_A"] = nc.dram_tensor("ffn_gate_A", [HID, R_LR], F32, kind="ExternalInput").ap()
    d["ffn_gate_B"] = nc.dram_tensor("ffn_gate_B", [R_LR, I_FFN], F32, kind="ExternalInput").ap()
    d["ffn_up_q"] = nc.dram_tensor("ffn_up_q", [I_FFN, HID], I32, kind="ExternalInput").ap()
    d["ffn_up_scale"] = nc.dram_tensor("ffn_up_scale", [I_FFN], F32, kind="ExternalInput").ap()
    d["ffn_up_A"] = nc.dram_tensor("ffn_up_A", [HID, R_LR], F32, kind="ExternalInput").ap()
    d["ffn_up_B"] = nc.dram_tensor("ffn_up_B", [R_LR, I_FFN], F32, kind="ExternalInput").ap()
    d["ffn_down_q"] = nc.dram_tensor("ffn_down_q", [HID, I_FFN], I32, kind="ExternalInput").ap()
    d["ffn_down_scale"] = nc.dram_tensor("ffn_down_scale", [HID], F32, kind="ExternalInput").ap()
    d["ffn_down_A"] = nc.dram_tensor("ffn_down_A", [I_FFN, R_LR], F32, kind="ExternalInput").ap()
    d["ffn_down_B"] = nc.dram_tensor("ffn_down_B", [R_LR, HID], F32, kind="ExternalInput").ap()
    d["out"] = nc.dram_tensor("out", [T, HID], F32, kind="ExternalOutput").ap()
    spill_kind = lambda nm: "ExternalOutput" if nm in debug_taps else "Internal"
    d["sp_y1"] = nc.dram_tensor("sp_y1", [T, HID], F32, kind=spill_kind("sp_y1")).ap()
    d["sp_yt"] = nc.dram_tensor("sp_yt", [HID, T], F32, kind=spill_kind("sp_yt")).ap()

    p = Prog()
    sb_ = {}

    def sb(name, shape, dt, **kw):
        t = ctx.enter_context(nc.sbuf_tensor(name, shape, dt, **kw))
        sb_[name] = t
        return t

    def ps(name, shape, dt=F32):
        t = ctx.enter_context(nc.psum_tensor(name, shape, dt))
        return t

    def dbg(name, ap, shape, dt):
        if name in debug_taps:
            o = nc.dram_tensor("dbg_" + name, shape, dt, kind="ExternalOutput").ap()
            tgt = p.dma("sync", o, ap, sm["st"])
            p.wait("sync", sm["st"], tgt)

    o_T = sb("o_T", [128, H * T], BF16)
    ar2 = sb("ar2", [128, DT * (TH + HALO)], BF16, is_transpose_dest=True)
    qTh = sb("qTh", [128, T], BF16)
    kTh = sb("kTh", [128, TH], BF16)
    skh = sb("skh", [128, TH], BF16)
    v_nt = sb("v_nt", [128, (NT + 1) * 128], BF16, is_transpose_dest=True)
    v_gla = sb("v_gla", [128, NT * 128], BF16, is_transpose_dest=True)
    vTst = sb("vTst", [128, TH + HALO], BF16)
    ks_h = sb("ks_h", [128, T], BF16)
    gs_h = sb("gs_h", [128, T], F32)
    S_f = sb("S_f", [128, KVH * Dh], F32)
    S_b = sb("S_b", [128, KVH * Dh], BF16)
    wst = sb("wst", [128, DT * 128], BF16)
    wst2 = sb("wst2", [128, DT * 128], BF16, is_transpose_dest=True)
    scr = sb("scr", [128, 1536], F32)
    scrb = sb("scrb", [128, 1024], BF16)
    red = sb("red", [128, 16], F32)
    cst = {}
    for name, shape, dt in CONST_SPECS:
        cst[name] = sb("c_" + name, shape, dt)

    sm = {k: Sem(nc, ctx, k) for k in
          ("ldw", "ldx", "ldh", "lw1", "lw2", "lda", "mm", "ev", "sc", "tr", "st", "ph", "pf")}

    def barrier():
        tgt = sm["ph"].n + 5
        for e in ENGINES:
            p.on(e, lambda en, s=sm["ph"].h: en.sem_inc(s, 1))
        sm["ph"].n = tgt
        for e in ENGINES:
            p.wait(e, sm["ph"], tgt)

    for name, shape, dt in CONST_SPECS:
        p.dma("sync", cst[name][:], d[name][:], sm["ldh"])
    for e in ENGINES:
        p.wait(e, sm["ldh"])
    p.op("vector", lambda e: e.memset(S_f[:], 0.0), sm["ev"])
    p.op("vector", lambda e: e.memset(red[:, 9:10], EPS), sm["ev"])
    p.op("vector", lambda e: e.memset(S_b[:], 0.0), sm["ev"])
    p.op("vector", lambda e: e.memset(vTst[:], 0.0), sm["ev"])
    barrier()

    def build_xT(dst, dst_stride, src_ap, rows, tag):
        ntile = (rows + 127) // 128
        for tt in range(ntile):
            r0 = tt * 128
            nr = min(128, rows - r0)
            stg = scrb
            p.dma("gpsimd", wst[0:nr, 0:4096], src_ap[r0:r0 + nr, :], sm["lw1"])
            p.wait("sync", sm["lw1"])
            if nr < 128:
                p.wait("vector", sm["lw1"])
                p.op("vector", lambda e, a=wst[nr:128, 0:4096]: e.memset(a, 0.0), sm["ev"])
                p.wait("sync", sm["ev"])
            src3 = wst[:, 0:4096].rearrange("p (dd f) -> p dd f", f=128)
            dst3 = dst[:, 0:DT * dst_stride].rearrange("p (dd t) -> p dd t", t=dst_stride)[:, :, r0:r0 + 128]
            tgt = p.dmat("sync", dst3, src3, sm["tr"])
            p.wait("gpsimd", sm["tr"], tgt)
        p.wait("tensor", sm["tr"])
        p.wait("vector", sm["tr"])

    kt_all = sb("kt_all", [128, NT * 128], BF16)
    ea_all = sb("ea_all", [128, NT * 128], F32)
    S_snap = sb("S_snap", [128, (NT + 1) * 128], BF16)
    sqh = sb("sqh", [128, T], BF16)

    pA = ps("pA", [128, 512])
    pB = ps("pB", [128, 512])
    pC = ps("pC", [128, 512])
    pD = ps("pD", [128, 512])
    pE = ps("pE", [128, 512])
    pF = ps("pF", [128, 1024], BF16)
    pG = ps("pG", [128, 512])
    pH = ps("pH", [128, 1024], BF16)

    ISQ = 1.0 / float(np.sqrt(128.0))
    XTS = TH + HALO

    def load_wstrip(dst, wd, col0, width=128, rows=HID, after=None):
        if after is None:
            p.wait("gpsimd", sm["mm"])
        elif after:
            p.wait("gpsimd", sm[after[0]], after[1])
        sem = sm["lw1"] if dst is wst else sm["lw2"]
        tg = None
        rchunk = 1024
        for r0 in range(0, rows, rchunk):
            src = wd[r0:r0 + rchunk, col0:col0 + width].rearrange("(dd p) o -> p dd o", p=128)
            c0 = (r0 // 128) * width
            dstc = dst[:, c0:c0 + (rchunk // 128) * width].rearrange("p (dd o) -> p dd o", o=width)
            tg = p.dma("gpsimd", dstc, src, sem)
        return tg

    def proj_strip(wdst, xts, rhs_base_off, ntok, out_cb, ld_tgt):
        p.wait("tensor", sm["lw1"] if wdst is wst else sm["lw2"], ld_tgt)
        nch = (ntok + 511) // 512
        sc_done = []
        for c in range(nch):
            t0c = c * 512
            n = min(512, ntok - t0c)
            pb_ = pA if c % 2 == 0 else pB
            if c >= 2:
                p.wait("tensor", sm["sc"], sc_done[c - 2])
            for dd in range(DT):
                lhs_ap = wdst[:, dd * 128:(dd + 1) * 128]
                rhs_ap = ar2[:, dd * xts + rhs_base_off + t0c: dd * xts + rhs_base_off + t0c + n]
                p.op("tensor",
                     lambda e, pb2=pb_, L=lhs_ap, R=rhs_ap, s=(dd == 0), st=(dd == DT - 1), n2=n:
                     e.matmul(pb2[:, 0:n2], L, R, start=s, stop=st),
                     sem=sm["mm"] if dd == DT - 1 else None)
            out_cb(pb_, c, t0c, n)
            sc_done.append(sm["sc"].n)

    def rope_emit(dst, srcF, c0, ncols, tab0):
        co, si = cst["cos_t"], cst["sin_t"]
        sl = slice(c0, c0 + ncols)
        tl = slice(tab0, tab0 + ncols)
        seq = [
            lambda e: e.tensor_copy(dst[0:64, sl], srcF[64:128, sl]),
            lambda e: e.tensor_mul(scr[0:64, 0:ncols], dst[0:64, sl], si[0:64, tl]),
            lambda e: e.tensor_mul(dst[0:64, sl], srcF[0:64, sl], co[0:64, tl]),
            lambda e: e.tensor_sub(dst[0:64, sl], dst[0:64, sl], scr[0:64, 0:ncols]),
            lambda e: e.tensor_copy(dst[64:128, sl], srcF[0:64, sl]),
            lambda e: e.tensor_mul(scr[64:128, 0:ncols], dst[64:128, sl], si[64:128, tl]),
            lambda e: e.tensor_mul(dst[64:128, sl], srcF[64:128, sl], co[64:128, tl]),
            lambda e: e.tensor_add(dst[64:128, sl], dst[64:128, sl], scr[64:128, 0:ncols]),
        ]
        for ii, f in enumerate(seq):
            if ii == len(seq) - 1:
                p.op("vector", f, sm["ev"])
            else:
                p.op("vector", f)
                p.drain("vector")

    def kv_pipeline(kvh, prefix, koff, xts):
        ntok = koff + T
        lt = load_wstrip(wst, d["attn_Wk"], kvh * 128)

        def ev_k(pb_, c, rel0, n):
            tgt = sm["mm"].n
            p.wait("scalar", sm["mm"], tgt)
            p.op("scalar", lambda e, pb2=pb_, r2=rel0, n2=n:
                 e.copy(kTh[:, r2:r2 + n2], pb2[:, 0:n2]), sm["sc"])
        proj_strip(wst, xts, 0, ntok, ev_k, lt)
        if not prefix and kvh == KVH - 1:
            p.wait("sync", sm["sc"])
            dbg("kTh_early", kTh[:], [128, TH], BF16)
        lt = load_wstrip(wst2, d["attn_Wv"], kvh * 128)
        p.wait("tensor", sm["sc"])

        def ev_v(pb_, c, rel0, n):
            tgt = sm["mm"].n
            p.wait("scalar", sm["mm"], tgt)
            p.op("scalar", lambda e, pb2=pb_, r2=rel0, n2=n:
                 e.copy(vTst[:, r2:r2 + n2], pb2[:, 0:n2]), sm["sc"])
        proj_strip(wst2, xts, 0, ntok, ev_v, lt)
        p.wait("sync", sm["sc"])
        src3 = vTst[:, koff:koff + NT * 128].rearrange("p (g f) -> p g f", f=128)
        dst3 = v_gla[:].rearrange("p (g f) -> p g f", f=128)
        tg_vg = p.dmat("sync", dst3, src3, sm["tr"])
        if not prefix:
            src3 = vTst[:, 0:(NT + 1) * 128].rearrange("p (g f) -> p g f", f=128)
            dst3 = v_nt[:].rearrange("p (g f) -> p g f", f=128)
            tg_vg = p.dmat("sync", dst3, src3, sm["tr"])
            p.wait("vector", sm["sc"])
            for cc in range(0, ntok, 512):
                nn = min(512, ntok - cc)
                rope_emit(skh, kTh, cc, nn, cc)
        p.wait("tensor", sm["sc"])
        for tt in range(NT):
            p.op("tensor", lambda e, t2=tt: e.transpose(
                pF[:, 0:128], kTh[:, koff + t2 * 128: koff + (t2 + 1) * 128], cst["ident_b"][:]), sm["mm"])
            p.wait("scalar", sm["mm"])
            p.op("scalar", lambda e, t2=tt: e.copy(gs_h[:, t2 * 128:(t2 + 1) * 128], pF[:, 0:128]), sm["sc"])
            p.wait("tensor", sm["sc"])
        p.drain("scalar")
        p.wait("vector", sm["sc"])
        for tt in range(NT):
            sl = slice(tt * 128, (tt + 1) * 128)
            p.op("vector", lambda e, s=sl: e.reduce_max(red[:, 0:1], gs_h[:, s], axis=AX.X, negate=True), sm["ev"])
            p.wait("scalar", sm["ev"])
            p.op("scalar", lambda e, s=sl: e.activation(
                scr[:, 0:128], gs_h[:, s], AF.Exp, bias=red[:, 0:1], accum_out=red[:, 1:2]), sm["sc"])
            p.wait("vector", sm["sc"])
            p.op("vector", lambda e: e.reciprocal(red[:, 2:3], red[:, 1:2]))
            p.drain("vector")
            p.op("vector", lambda e, s=sl: e.tensor_scalar(
                ks_h[:, s], scr[:, 0:128], red[:, 2:3], None, ALU.mult), sm["ev"])
        p.wait("scalar", sm["ev"])
        p.op("scalar", lambda e: e.activation(scr[:, 0:1024], gs_h[:, 0:1024], AF.Sigmoid), sm["sc"])
        p.drain("scalar")
        p.op("scalar", lambda e: e.activation(gs_h[:, 0:1024], scr[:, 0:1024], AF.Ln), sm["sc"])
        if not prefix and kvh == KVH - 1:
            p.wait("sync", sm["sc"])
            p.wait("sync", sm["ev"])
            dbg("kTh_mid", kTh[:], [128, TH], BF16)
        p.wait("tensor", sm["ev"])
        p.wait("tensor", sm["sc"])
        p.wait("tensor", sm["tr"], tg_vg)
        Scur = S_f[:, kvh * 128:(kvh + 1) * 128]
        if not prefix:
            p.op("vector", lambda e: e.tensor_copy(S_snap[:, 0:128], Scur), sm["ev"])
        for tt in range(NT):
            gsl = gs_h[:, tt * 128:(tt + 1) * 128]
            ksl = ks_h[:, tt * 128:(tt + 1) * 128]
            p.op("tensor", lambda e, g=gsl: e.matmul(pC[:, 0:128], cst["tri_rev"][:], g, start=True, stop=True))
            p.op("tensor", lambda e, g=gsl: e.matmul(pD[:, 0:128], g, cst["tri_incl"][:], start=True, stop=True))
            p.op("tensor", lambda e, g=gsl: e.matmul(pC[:, 128:256], cst["tri_incl"][:], g, start=True, stop=True), sm["mm"])
            mm0 = sm["mm"].n
            p.wait("scalar", sm["mm"], mm0)
            p.op("scalar", lambda e: e.activation(scr[:, 0:128], pC[:, 0:128], AF.Exp))
            p.drain("scalar")
            p.op("scalar", lambda e, t2=tt: e.activation(ea_all[:, t2 * 128:(t2 + 1) * 128], pC[:, 128:256], AF.Exp))
            p.drain("scalar")
            p.op("scalar", lambda e: e.activation(scr[:, 128:256], pC[:, 128:256], AF.Exp, scale=-1.0))
            p.drain("scalar")
            p.op("scalar", lambda e: e.activation(red[:, 4:5], pD[:, 127:128], AF.Exp), sm["sc"])
            sc0 = sm["sc"].n
            p.wait("vector", sm["sc"], sc0)
            p.op("vector", lambda e, k=ksl: e.tensor_mul(scrb[:, 0:128], k, scr[:, 0:128]))
            p.drain("vector")
            p.op("vector", lambda e, k=ksl: e.tensor_mul(scrb[:, 128:256], k, scr[:, 128:256]), sm["ev"])
            ev0 = sm["ev"].n
            p.wait("tensor", sm["ev"], ev0)
            p.op("tensor", lambda e: e.transpose(pF[:, 0:128], scrb[:, 128:256], cst["ident_b"][:]), sm["mm"])
            p.wait("vector", sm["mm"])
            p.op("vector", lambda e, t2=tt: e.tensor_copy(kt_all[:, t2 * 128:(t2 + 1) * 128], pF[:, 0:128]), sm["ev"])
            p.op("tensor", lambda e, t2=tt: e.matmul(
                pE[:, 0:128], scrb[:, 0:128], v_gla[:, t2 * 128:(t2 + 1) * 128], start=True, stop=True), sm["mm"])
            mm1 = sm["mm"].n
            p.wait("vector", sm["mm"], mm1)
            p.op("vector", lambda e, S=Scur: e.scalar_tensor_tensor(
                S, S, red[:, 4:5], pE[:, 0:128], ALU.mult, ALU.add))
            p.drain("vector")
            if not prefix:
                p.op("vector", lambda e, t2=tt, S=Scur: e.tensor_copy(
                    S_snap[:, (t2 + 1) * 128:(t2 + 2) * 128], S), sm["ev"])
            else:
                p.op("vector", lambda e: e.engine_nop() if False else e.tensor_copy(red[:, 8:9], red[:, 4:5]), sm["ev"])
            evS = sm["ev"].n
            p.wait("tensor", sm["ev"], evS)
        if prefix:
            return
        for qh in range(GRP):
            hq = kvh * GRP + qh
            lt = load_wstrip(wst, d["attn_Wq"], hq * 128)

            def ev_q(pb_, c, rel0, n):
                tgt = sm["mm"].n
                p.wait("scalar", sm["mm"], tgt)
                p.op("scalar", lambda e, pb2=pb_, r2=rel0, n2=n:
                     e.copy(qTh[:, r2:r2 + n2], pb2[:, 0:n2]), sm["sc"])
            proj_strip(wst, xts, HALO, T, ev_q, lt)
            p.wait("vector", sm["sc"])
            for cc in range(0, T, 512):
                nn = min(512, T - cc)
                rope_emit(sqh, qTh, cc, nn, HALO + cc)
            p.wait("tensor", sm["ev"])
            for tt in range(NT):
                p.op("tensor", lambda e, t2=tt: e.transpose(
                    pF[:, 0:128], qTh[:, t2 * 128:(t2 + 1) * 128], cst["ident_b"][:]), sm["mm"])
                mmq = sm["mm"].n
                p.wait("vector", sm["mm"], mmq)
                p.op("vector", lambda e: e.reduce_max(red[:, 0:1], pF[:, 0:128], axis=AX.X, negate=True), sm["ev"])
                p.wait("scalar", sm["ev"])
                p.op("scalar", lambda e: e.activation(
                    scr[:, 0:128], pF[:, 0:128], AF.Exp, bias=red[:, 0:1], accum_out=red[:, 1:2]), sm["sc"])
                p.wait("vector", sm["sc"])
                p.op("vector", lambda e: e.reciprocal(red[:, 2:3], red[:, 1:2]))
                p.drain("vector")
                p.op("vector", lambda e: e.tensor_scalar(
                    scr[:, 128:256], scr[:, 0:128], red[:, 2:3], 0.5, ALU.mult, ALU.mult))
                p.drain("vector")
                p.op("vector", lambda e, t2=tt: e.tensor_mul(
                    scrb[:, 0:128], scr[:, 128:256], ea_all[:, t2 * 128:(t2 + 1) * 128]), sm["ev"])
                ev1 = sm["ev"].n
                p.wait("tensor", sm["ev"], ev1)
                p.op("tensor", lambda e: e.transpose(pH[:, 0:128], scrb[:, 0:128], cst["ident_b"][:]), sm["mm"])
                p.wait("vector", sm["mm"])
                p.op("vector", lambda e: e.tensor_copy(scrb[:, 256:384], pH[:, 0:128]), sm["ev"])
                ev2 = sm["ev"].n
                p.wait("tensor", sm["ev"], ev2)
                p.op("tensor", lambda e, t2=tt: e.matmul(
                    pE[:, 0:128], kt_all[:, t2 * 128:(t2 + 1) * 128], scrb[:, 256:384],
                    start=True, stop=True), sm["mm"])
                p.wait("vector", sm["mm"])
                p.op("vector", lambda e: e.tensor_mul(scrb[:, 384:512], pE[:, 0:128], cst["mask_tri"][:]), sm["ev"])
                ev3 = sm["ev"].n
                p.op("tensor", lambda e, t2=tt: e.matmul(
                    pA[:, 0:192], sqh[:, t2 * 128:(t2 + 1) * 128], skh[:, t2 * 128:t2 * 128 + 192],
                    start=True, stop=True), sm["mm"])
                mmS = sm["mm"].n
                p.wait("vector", sm["mm"], mmS)
                mtile = cst["mask_first"] if tt == 0 else cst["mask_rest"]
                p.op("vector", lambda e, m=mtile: e.scalar_tensor_tensor(
                    scr[:, 256:448], pA[:, 0:192], ISQ, m[:], ALU.mult, ALU.add))
                p.drain("vector")
                p.op("vector", lambda e: e.reduce_max(red[:, 0:1], scr[:, 256:448], axis=AX.X, negate=True), sm["ev"])
                p.wait("scalar", sm["ev"])
                p.op("scalar", lambda e: e.activation(
                    scr[:, 448:640], scr[:, 256:448], AF.Exp, bias=red[:, 0:1], accum_out=red[:, 1:2]), sm["sc"])
                p.wait("vector", sm["sc"])
                p.op("vector", lambda e: e.reciprocal(red[:, 2:3], red[:, 1:2]))
                p.drain("vector")
                p.op("vector", lambda e: e.tensor_scalar(
                    scrb[:, 512:704], scr[:, 448:640], red[:, 2:3], 0.5, ALU.mult, ALU.mult), sm["ev"])
                ev4 = sm["ev"].n
                p.wait("tensor", sm["ev"], ev4)
                p.op("tensor", lambda e: e.transpose(pH[:, 128:256], scrb[:, 512:640], cst["ident_b"][:]))
                p.op("tensor", lambda e: e.transpose(pH[0:64, 256:384], scrb[:, 640:704], cst["ident_b"][:]), sm["mm"])
                mmT = sm["mm"].n
                p.wait("vector", sm["mm"], mmT)
                p.op("vector", lambda e: e.tensor_copy(scrb[:, 704:832], pH[:, 128:256]))
                p.drain("vector")
                p.op("vector", lambda e: e.tensor_copy(scrb[0:64, 832:960], pH[0:64, 256:384]), sm["ev"])
                ev5 = sm["ev"].n
                p.wait("tensor", sm["ev"], ev5)
                p.op("tensor", lambda e, t2=tt: e.matmul(
                    pG[:, 0:128], S_snap[:, t2 * 128:(t2 + 1) * 128], scrb[:, 256:384],
                    start=True, stop=False))
                p.op("tensor", lambda e, t2=tt: e.matmul(
                    pG[:, 0:128], v_gla[:, t2 * 128:(t2 + 1) * 128], scrb[:, 384:512],
                    start=False, stop=False))
                p.op("tensor", lambda e, t2=tt: e.matmul(
                    pG[:, 0:128], v_nt[:, t2 * 128:(t2 + 1) * 128], scrb[:, 704:832],
                    start=False, stop=False))
                p.op("tensor", lambda e, t2=tt: e.matmul(
                    pG[:, 0:128], v_nt[0:64, (t2 + 1) * 128:(t2 + 2) * 128], scrb[0:64, 832:960],
                    start=False, stop=True), sm["mm"])
                mmO = sm["mm"].n
                p.wait("scalar", sm["mm"], mmO)
                p.op("scalar", lambda e, t2=tt, h2=hq: e.copy(
                    o_T[:, h2 * T + t2 * 128: h2 * T + (t2 + 1) * 128], pG[:, 0:128]), sm["sc"])
                p.wait("tensor", sm["sc"])

    for pb in range(PFB):
        p.op("vector", lambda e: e.memset(ar2[:, 0:DT * 1024], 0.0), sm["ev"])
        p.wait("sync", sm["ev"])
        p.wait("gpsimd", sm["ev"])
        build_xT(ar2, 1024, d["x_prefix"][pb * T:(pb + 1) * T, :], T, f"xp{pb}")
        barrier()
        for kvh in range(KVH):
            kv_pipeline(kvh, prefix=True, koff=0, xts=1024)
            barrier()

    p.op("vector", lambda e: e.memset(ar2[:], 0.0), sm["ev"])
    p.wait("sync", sm["ev"])
    p.wait("gpsimd", sm["ev"])
    build_xT(ar2, XTS, d["x_main"], TH, "xmain")
    barrier()
    dbg("xT", ar2[:], [128, DT * (TH + HALO)], BF16)

    for kvh in range(KVH):
        kv_pipeline(kvh, prefix=False, koff=HALO, xts=XTS)
        barrier()
    dbg("o_T", o_T[:], [128, H * T], BF16)
    dbg("S_f", S_f[:], [128, KVH * 128], F32)
    dbg("kTh", kTh[:], [128, TH], BF16)
    dbg("skh", skh[:], [128, TH], BF16)
    dbg("ks_h", ks_h[:], [128, T], BF16)
    dbg("gs_h", gs_h[:], [128, T], F32)
    dbg("ea_all", ea_all[:], [128, NT * 128], F32)
    dbg("kt_all", kt_all[:], [128, NT * 128], BF16)
    dbg("S_snap", S_snap[:], [128, (NT + 1) * 128], BF16)
    dbg("sqh", sqh[:], [128, T], BF16)
    dbg("qTh", qTh[:], [128, T], BF16)
    dbg("v_nt", v_nt[:], [128, (NT + 1) * 128], BF16)
    dbg("v_gla", v_gla[:], [128, NT * 128], BF16)


    if stop_after < 5:
        meta = dict(nc=nc, ctx=ctx, prog=p, sems=sm, dram=d, sbuf=sb_, cst=cst)
        return meta
    ar2f = ar2[:].bitcast(F32)
    Y1S = 4096

    def ln_tile(buf_ap, ncols, out_cb):
        p.op("vector", lambda e: e.reduce_sum(red[:, 0:1], buf_ap, axis=AX.X))
        p.drain("vector")
        p.op("vector", lambda e: e.tensor_scalar(red[:, 0:1], red[:, 0:1], 1.0 / ncols, None, ALU.mult), sm["ev"])
        p.wait("scalar", sm["ev"])
        wf0, wf1 = wst[:].bitcast(F32), wst2[:].bitcast(F32)
        half = ncols // 2
        p.wait("scalar", sm["tr"])
        p.op("scalar", lambda e: e.activation(wf0[:, 0:half], buf_ap[:, 0:half], AF.Square,
                                              accum_out=red[:, 1:2]))
        p.drain("scalar")
        p.op("scalar", lambda e: e.activation(wf1[:, 0:half], buf_ap[:, half:ncols], AF.Square,
                                              accum_out=red[:, 4:5]), sm["sc"])
        p.wait("vector", sm["sc"])
        p.op("vector", lambda e: e.tensor_add(red[:, 1:2], red[:, 1:2], red[:, 4:5]))
        p.drain("vector")
        p.op("vector", lambda e: e.tensor_scalar(red[:, 1:2], red[:, 1:2], 1.0 / ncols, None, ALU.mult))
        p.drain("vector")
        p.op("vector", lambda e: e.tensor_mul(red[:, 4:5], red[:, 0:1], red[:, 0:1]))
        p.drain("vector")
        p.op("vector", lambda e: e.tensor_sub(red[:, 1:2], red[:, 1:2], red[:, 4:5]), sm["ev"])
        p.wait("scalar", sm["ev"])
        p.op("scalar", lambda e: e.activation(red[:, 2:3], red[:, 1:2], AF.Ln, bias=red[:, 9:10]))
        p.drain("scalar")
        p.op("scalar", lambda e: e.activation(red[:, 2:3], red[:, 2:3], AF.Exp, scale=-0.5), sm["sc"])
        p.wait("vector", sm["sc"])
        p.op("vector", lambda e: e.tensor_scalar(buf_ap, buf_ap, red[:, 0:1], red[:, 2:3],
                                                 ALU.subtract, ALU.mult), sm["ev"])
        p.drain("vector")
        out_cb()

    for tc in range(TC):
        p.wait("vector", sm["st"])
        buf_busy = {0: ("mm", 0), 1: ("mm", 0)}
        for jt in range(DT):
            wbuf = wst if jt % 2 == 0 else wst2
            lt = load_wstrip(wbuf, d["attn_Wo"], jt * 128, after=buf_busy[jt % 2])
            p.wait("tensor", sm["lw1"] if jt % 2 == 0 else sm["lw2"], lt)
            pb_ = pA if jt % 2 == 0 else pB
            for ot in range(DT):
                lhs_ap = wbuf[:, ot * 128:(ot + 1) * 128]
                rhs_ap = o_T[:, ot * T + tc * 512: ot * T + tc * 512 + 512]
                p.op("tensor", lambda e, pb2=pb_, L=lhs_ap, R=rhs_ap,
                     s=(ot == 0), st=(ot == DT - 1):
                     e.matmul(pb2[:, 0:512], L, R, start=s, stop=st),
                     sem=sm["mm"] if ot == DT - 1 else None)
            buf_busy[jt % 2] = ("mm", sm["mm"].n)
            p.wait("vector", sm["mm"])
            slot = (jt % 2) * 512
            p.op("vector", lambda e, pb2=pb_, s0=slot: e.tensor_copy(scr[:, s0:s0 + 512], pb2[:, 0:512]), sm["ev"])
            ev0 = sm["ev"].n
            p.wait("tensor", sm["ev"], ev0)
            for q4 in range(4):
                p.op("tensor", lambda e, s0=slot, q2=q4: e.transpose(
                    pE[:, q2 * 128:(q2 + 1) * 128], scr[:, s0 + q2 * 128: s0 + (q2 + 1) * 128],
                    cst["ident_f"][:]), sem=sm["mm"] if q4 == 3 else None)
            mmT = sm["mm"].n
            p.wait("vector", sm["mm"], mmT)
            for q4 in range(4):
                dst = ar2f[:, q4 * Y1S + jt * 128: q4 * Y1S + (jt + 1) * 128]
                p.op("vector", lambda e, dd=dst, q2=q4: e.tensor_copy(dd, pE[:, q2 * 128:(q2 + 1) * 128]),
                     sem=sm["ev"] if q4 == 3 else None)
            evT = sm["ev"].n
            p.wait("tensor", sm["ev"], evT)
        p.wait("gpsimd", sm["ev"])
        p.wait("gpsimd", sm["mm"])
        for q4 in range(4):
            tglob = tc * 512 + q4 * 128
            wfs = [wst[:].bitcast(F32), wst2[:].bitcast(F32)]
            p.dma("gpsimd", wfs[0][:, 0:2048],
                  d["x_main"][HALO + tglob: HALO + tglob + 128, 0:2048], sm["lw1"])
            p.dma("gpsimd", wfs[1][:, 0:2048],
                  d["x_main"][HALO + tglob: HALO + tglob + 128, 2048:4096], sm["lw2"])
            p.wait("vector", sm["lw1"])
            p.wait("vector", sm["lw2"])
            for hh in range(2):
                sl = ar2f[:, q4 * Y1S + hh * 2048: q4 * Y1S + (hh + 1) * 2048]
                p.op("vector", lambda e, s=sl, h2=hh: e.tensor_add(s, s, wfs[h2][:, 0:2048]),
                     sem=sm["ev"] if hh == 1 else None)
            p.drain("vector")
            p.wait("gpsimd", sm["ev"])

            def emit_y1(q2=q4, tg2=tglob):
                p.wait("sync", sm["ev"])
                p.dma("sync", d["sp_y1"][tg2:tg2 + 128, :], ar2f[:, q2 * Y1S:(q2 + 1) * Y1S], sm["st"])
                p.wait("gpsimd", sm["ev"])
                p.op("vector", lambda e: e.tensor_copy(wst[:, 0:4096], ar2f[:, q2 * Y1S:(q2 + 1) * Y1S]), sm["ev"])
                evc = sm["ev"].n
                p.wait("sync", sm["ev"], evc)
                src3 = wst[:, 0:4096].rearrange("p (dd f) -> p dd f", f=128)
                dst3 = o_T[:].rearrange("p (dd t) -> p dd t", t=T)[:, :, tg2:tg2 + 128]
                tgt = p.dmat("sync", dst3, src3, sm["tr"])
                p.wait("gpsimd", sm["tr"], tgt)
                p.wait("vector", sm["tr"], tgt)
            ln_tile(ar2f[:, q4 * Y1S:(q4 + 1) * Y1S], HID, emit_y1)
    barrier()
    dbg("y1T", o_T[:], [128, DT * T], BF16)


    if stop_after < 6:
        meta = dict(nc=nc, ctx=ctx, prog=p, sems=sm, dram=d, sbuf=sb_, cst=cst)
        return meta
    y1T = o_T
    p.wait("tensor", sm["tr"])
    p.wait("vector", sm["tr"])
    p.wait("scalar", sm["tr"])
    BSG, BSU, BSD = 22528, 28032, 33536

    def bs_ap(off, oi):
        pb0 = 64 * (oi % 2)
        c0 = off + (oi // 2) * 128
        return ar2[pb0:pb0 + 33, c0:c0 + 128]

    for tc in range(TC):
        for dd in range(DT):
            p.op("tensor", lambda e, d2=dd, c2=tc: e.matmul(
                pA[0:1, 0:512], cst["ones_b"][:], y1T[:, d2 * T + c2 * 512: d2 * T + c2 * 512 + 512],
                start=(d2 == 0), stop=(d2 == DT - 1)), sem=sm["mm"] if dd == DT - 1 else None)
        p.wait("scalar", sm["mm"])
        p.op("scalar", lambda e, c2=tc: e.copy(skh[0:1, c2 * 512: c2 * 512 + 512], pA[0:1, 0:512]), sm["sc"])
        p.wait("vector", sm["sc"])
        for rdst in (scrb, qTh):
            for rr in (32, 96):
                p.op("vector", lambda e, c2=tc, rd=rdst, r2=rr: e.tensor_copy(
                    rd[r2:r2 + 1, c2 * 512:c2 * 512 + 512], skh[0:1, c2 * 512:c2 * 512 + 512]))
                p.drain("vector")
        p.op("vector", lambda e: e.tensor_copy(red[:, 10:11], red[:, 9:10]), sm["ev"])
        p.wait("tensor", sm["sc"])

    for which, wdram, rows0 in (("g", d["ffn_gate_A"], 0), ("u", d["ffn_up_A"], 32)):
        p.wait("vector", sm["sc"])
        lt = load_wstrip(wst, wdram, 0, width=32)
        p.wait("tensor", sm["lw1"], lt)
        tbuf = scrb if which == "g" else qTh
        for tc in range(TC):
            for dd in range(DT):
                p.op("tensor", lambda e, d2=dd, c2=tc: e.matmul(
                    pA[0:32, 0:512], wst[:, d2 * 32:(d2 + 1) * 32],
                    y1T[:, d2 * T + c2 * 512: d2 * T + c2 * 512 + 512],
                    start=(d2 == 0), stop=(d2 == DT - 1)), sem=sm["mm"] if dd == DT - 1 else None)
            p.wait("vector", sm["mm"])
            p.op("vector", lambda e, c2=tc, tb=tbuf: e.tensor_scalar(
                tb[0:32, c2 * 512:c2 * 512 + 512], pA[0:32, 0:512], 0.02, None, ALU.mult))
            p.drain("vector")
            p.op("vector", lambda e, c2=tc, tb=tbuf: e.tensor_scalar(
                tb[64:96, c2 * 512:c2 * 512 + 512], pA[0:32, 0:512], 0.02, None, ALU.mult), sm["ev"])
            p.wait("tensor", sm["ev"])

    def build_bs(bdram, sdram, n_tiles, dst_off):
        for oi in range(n_tiles):
            tg = p.dma("sync", scr[0:32, 0:128], bdram[0:32, oi * 128:(oi + 1) * 128], sm["ldh"])
            tg2 = p.dma("sync", red[:, 5:6],
                        sdram[oi * 128:(oi + 1) * 128].rearrange("(p o) -> p o", o=1), sm["ldh"])
            p.wait("vector", sm["ldh"], tg2)
            p.op("vector", lambda e: e.tensor_scalar(red[:, 6:7], red[:, 5:6], 0.005, None, ALU.mult))
            p.drain("vector")
            p.op("vector", lambda e: e.reciprocal(red[:, 7:8], red[:, 6:7]), sm["ev"])
            p.drain("vector")
            ev0 = sm["ev"].n
            p.wait("tensor", sm["ldh"])
            p.op("tensor", lambda e: e.transpose(pE[:, 0:32], scr[0:32, 0:128], cst["ident_f"][0:32, 0:32]), sm["mm"])
            mm0 = sm["mm"].n
            p.wait("vector", sm["mm"], mm0)
            p.op("vector", lambda e: e.tensor_scalar(vTst[:, 0:32], pE[:, 0:32], red[:, 7:8], None, ALU.mult), sm["ev"])
            ev1 = sm["ev"].n
            p.wait("tensor", sm["ev"], ev1)
            p.op("tensor", lambda e: e.transpose(pF[0:32, 0:128], vTst[:, 0:32], cst["ident_b"][:]), sm["mm"])
            mm1 = sm["mm"].n
            p.wait("scalar", sm["mm"], mm1)
            p.op("scalar", lambda e, o2=oi: e.copy(
                ar2[64 * (o2 % 2):64 * (o2 % 2) + 32, dst_off + (o2 // 2) * 128: dst_off + (o2 // 2) * 128 + 128],
                pF[0:32, 0:128]), sm["sc"])
            p.wait("tensor", sm["sc"])
            p.wait("sync", sm["sc"])
    p.wait("vector", sm["st"])
    p.wait("scalar", sm["st"])
    p.op("vector", lambda e: e.memset(ar2[32:33, BSG:BSG + 13056], -7.5))
    p.drain("vector")
    p.op("vector", lambda e: e.memset(ar2[96:97, BSG:BSG + 13056], -7.5), sm["ev"])
    p.wait("sync", sm["ev"])
    p.wait("scalar", sm["ev"])
    build_bs(d["ffn_gate_B"], d["ffn_gate_scale"], IT, BSG)
    build_bs(d["ffn_up_B"], d["ffn_up_scale"], IT, BSU)
    build_bs(d["ffn_down_B"], d["ffn_down_scale"], DT, BSD)
    p.op("vector", lambda e: e.memset(kTh[:, 32:33], 1.0), sm["ev"])
    barrier()

    if stop_after < 6.5:
        meta = dict(nc=nc, ctx=ctx, prog=p, sems=sm, dram=d, sbuf=sb_, cst=cst)
        return meta
    QW = [22, 22, 21, 21]
    QB = [0, 22, 44, 65]
    t3_open = [False]

    def ffn_quarter(qt, parts="ab", maxoi=99, skips=""):
        qw, qb = QW[qt], QB[qt]
        p.wait("vector", sm["st"])
        p.wait("scalar", sm["st"])
        wst_free = sm["tr"].n
        kth_free = sm["mm"].n
        for ol in range(min(qw, maxoi) if "a" in parts else 0):
            oi = qb + ol
            p.wait("gpsimd", sm["tr"], wst_free)
            ltg = p.dma("gpsimd", wst[:, 0:4096], d["ffn_gate_q"][oi * 128:(oi + 1) * 128, :], sm["lw1"])
            p.wait("sync", sm["lw1"], ltg)
            p.wait("sync", sm["mm"])
            src3 = wst[:, 0:4096].rearrange("p (dd f) -> p dd f", f=128)
            dst3 = wst2[:, 0:4096].rearrange("p (dd f) -> p dd f", f=128)
            tgt = p.dmat("sync", dst3, src3, sm["tr"])
            p.wait("sync", sm["ev"])
            tgs = p.dma("sync", red[:, 5:6],
                        d["ffn_gate_scale"][oi * 128:(oi + 1) * 128].rearrange("(p o) -> p o", o=1), sm["ldh"])
            tgs2 = p.dma("sync", red[:, 6:7],
                         d["ffn_up_scale"][oi * 128:(oi + 1) * 128].rearrange("(p o) -> p o", o=1), sm["ldh"])
            p.wait("vector", sm["ldh"], tgs2)
            p.op("vector", lambda e: e.tensor_scalar(red[:, 5:6], red[:, 5:6], 0.005, None, ALU.mult))
            p.drain("vector")
            p.op("vector", lambda e: e.tensor_scalar(red[:, 6:7], red[:, 6:7], 0.005, None, ALU.mult), sm["ev"])
            p.wait("gpsimd", sm["mm"], kth_free)
            p.wait("gpsimd", sm["lda"])
            tga = p.dma("gpsimd", kTh[:, 0:32], d["ffn_down_A"][oi * 128:(oi + 1) * 128, :], sm["lda"])
            p.wait("tensor", sm["tr"], tgt)
            for tc in range(TC):
                pb_ = pA if tc == 0 else pB
                pr0 = 64 * (oi % 2)
                p.op("tensor", lambda e, pb2=pb_, o2=oi, c2=tc, r0=pr0: e.matmul(
                    pb2[:, 0:512], bs_ap(BSG, o2),
                    scrb[r0:r0 + 33, c2 * 512:c2 * 512 + 512], start=True, stop=False))
                for dd in range(DT):
                    p.op("tensor", lambda e, pb2=pb_, d2=dd, c2=tc: e.matmul(
                        pb2[:, 0:512], wst2[:, d2 * 128:(d2 + 1) * 128],
                        y1T[:, d2 * T + c2 * 512: d2 * T + c2 * 512 + 512],
                        start=False, stop=(d2 == DT - 1)),
                        sem=sm["mm"] if dd == DT - 1 else None)
            mmg = sm["mm"].n
            p.wait("gpsimd", sm["tr"], tgt)
            ltu = p.dma("gpsimd", wst[:, 0:4096], d["ffn_up_q"][oi * 128:(oi + 1) * 128, :], sm["lw1"])
            p.wait("sync", sm["lw1"], ltu)
            p.wait("sync", sm["mm"], mmg)
            tgt2 = p.dmat("sync", dst3, src3, sm["tr"])
            wst_free = tgt2
            p.wait("tensor", sm["tr"], tgt2)
            for tc in range(TC):
                pb_ = pC if tc == 0 else pD
                pr0 = 64 * (oi % 2)
                p.op("tensor", lambda e, pb2=pb_, o2=oi, c2=tc, r0=pr0: e.matmul(
                    pb2[:, 0:512], bs_ap(BSU, o2),
                    qTh[r0:r0 + 33, c2 * 512:c2 * 512 + 512], start=True, stop=False))
                for dd in range(DT):
                    p.op("tensor", lambda e, pb2=pb_, d2=dd, c2=tc: e.matmul(
                        pb2[:, 0:512], wst2[:, d2 * 128:(d2 + 1) * 128],
                        y1T[:, d2 * T + c2 * 512: d2 * T + c2 * 512 + 512],
                        start=False, stop=(d2 == DT - 1)),
                        sem=sm["mm"] if dd == DT - 1 else None)
            mmu = sm["mm"].n
            for tc in range(TC):
                pg_ = pA if tc == 0 else pB
                pu_ = pC if tc == 0 else pD
                p.wait("vector", sm["mm"], mmu)
                p.wait("vector", sm["ev"])
                p.op("vector", lambda e, pg2=pg_: e.tensor_scalar(
                    scr[:, 0:512], pg2[:, 0:512], red[:, 5:6], None, ALU.mult), sm["ev"])
                p.wait("scalar", sm["ev"])
                p.op("scalar", lambda e: e.activation(scr[:, 512:1024], scr[:, 0:512], AF.Silu), sm["sc"])
                p.op("vector", lambda e, pu2=pu_: e.tensor_scalar(
                    scr[:, 1024:1536], pu2[:, 0:512], red[:, 6:7], None, ALU.mult))
                p.drain("vector")
                p.wait("vector", sm["sc"])
                p.op("vector", lambda e, o2=ol, c2=tc: e.tensor_mul(
                    ar2[:, o2 * T + c2 * 512: o2 * T + c2 * 512 + 512],
                    scr[:, 512:1024], scr[:, 1024:1536]), sm["ev"])
            evh = sm["ev"].n
            p.wait("tensor", sm["ev"], evh)
            p.wait("tensor", sm["lda"])
            for tc in range(TC if "t" not in skips else 0):
                pt_ = pE if tc == 0 else pG
                p.op("tensor", lambda e, pt2=pt_, o2=ol, c2=tc: e.matmul(
                    pt2[0:33, 0:512], kTh[:, 0:33],
                    ar2[:, o2 * T + c2 * 512: o2 * T + c2 * 512 + 512],
                    start=True, stop=True), sem=sm["mm"])
                mmt = sm["mm"].n
                p.wait("vector", sm["mm"], mmt)
                if qt == 0 and ol == 0:
                    p.op("vector", lambda e, pt2=pt_, c2=tc: e.tensor_copy(
                        ea_all[0:33, c2 * 512:c2 * 512 + 512], pt2[0:33, 0:512]), sm["ev"])
                else:
                    p.op("vector", lambda e, pt2=pt_, c2=tc: e.tensor_add(
                        ea_all[0:33, c2 * 512:c2 * 512 + 512],
                        ea_all[0:33, c2 * 512:c2 * 512 + 512], pt2[0:33, 0:512]), sm["ev"])
                evt = sm["ev"].n
                p.wait("tensor", sm["ev"], evt)
            kth_free = sm["mm"].n
        p.wait("sync", sm["ev"])
        dbg(f"hm_q{qt}", ar2[:, 0:qw * T], [128, qw * T], BF16)
        if qt == 3:
            for tc in range(TC):
                sl = slice(tc * 512, tc * 512 + 512)
                p.op("vector", lambda e, s=sl: e.tensor_scalar(
                    sqh[0:32, s], ea_all[0:32, s], 0.02, None, ALU.mult))
                p.drain("vector")
                p.op("vector", lambda e, s=sl: e.tensor_scalar(
                    sqh[64:96, s], ea_all[0:32, s], 0.02, None, ALU.mult))
                p.drain("vector")
                p.op("vector", lambda e, s=sl: e.tensor_copy(sqh[32:33, s], ea_all[32:33, s]))
                p.drain("vector")
                p.op("vector", lambda e, s=sl: e.tensor_copy(sqh[96:97, s], ea_all[32:33, s]), sm["ev"])
            p.wait("tensor", sm["ev"])
            p.wait("sync", sm["ev"])
            dbg("t3T", sqh[:], [128, T], BF16)
            dbg("rrow", skh[:], [128, TH], BF16)
            dbg("t1g", scrb[:], [128, 1024], BF16)
            dbg("t1u", qTh[:], [128, T], BF16)
        iw = qw * 128
        wst_free = sm["tr"].n
        for jglob in range(DT if "b" in parts else 0):
            j0 = jglob * 128
            p.wait("gpsimd", sm["tr"], wst_free)
            ltd = p.dma("gpsimd", wst[:, 0:iw],
                        d["ffn_down_q"][j0:j0 + 128, qb * 128: qb * 128 + iw], sm["lw1"])
            p.wait("sync", sm["lw1"], ltd)
            p.wait("sync", sm["mm"])
            src3 = wst[:, 0:iw].rearrange("p (it f) -> p it f", f=128)
            dst3 = wst2[:, 0:iw].rearrange("p (it f) -> p it f", f=128)
            tgt = p.dmat("sync", dst3, src3, sm["tr"])
            wst_free = tgt
            p.wait("tensor", sm["tr"], tgt)
            for tc in range(TC):
                pb_ = pA if tc == 0 else pB
                if qt == 3:
                    pr0 = 64 * (jglob % 2)
                    p.op("tensor", lambda e, pb2=pb_, jg2=jglob, c2=tc, r0=pr0: e.matmul(
                        pb2[:, 0:512], bs_ap(BSD, jg2),
                        sqh[r0:r0 + 33, c2 * 512:c2 * 512 + 512], start=True, stop=False))
                for ol in range(qw):
                    last = (ol == qw - 1)
                    p.op("tensor", lambda e, pb2=pb_, o2=ol, c2=tc, l2=last, s2=(ol == 0 and qt < 3): e.matmul(
                        pb2[:, 0:512], wst2[:, o2 * 128:(o2 + 1) * 128],
                        ar2[:, o2 * T + c2 * 512: o2 * T + c2 * 512 + 512],
                        start=s2, stop=l2),
                        sem=sm["mm"] if last else None)
            mmd = sm["mm"].n
            p.wait("vector", sm["mm"], mmd)
            for tc in range(TC):
                pb_ = pA if tc == 0 else pB
                if qt > 0:
                    tgl = p.dma("gpsimd", gs_h[:, tc * 512:tc * 512 + 512],
                                d["sp_yt"][j0:j0 + 128, tc * 512:tc * 512 + 512], sm["ldx"])
                    p.wait("vector", sm["ldx"], tgl)
                    p.op("vector", lambda e, pb2=pb_, c2=tc: e.tensor_add(
                        scr[:, 0:512], pb2[:, 0:512], gs_h[:, c2 * 512:c2 * 512 + 512]))
                    p.drain("vector")
                else:
                    p.op("vector", lambda e, pb2=pb_: e.tensor_copy(scr[:, 0:512], pb2[:, 0:512]))
                    p.drain("vector")
                if qt == 3:
                    tgs = p.dma("sync", red[:, 5:6],
                                d["ffn_down_scale"][j0:j0 + 128]
                                .rearrange("(p o) -> p o", o=1), sm["ldh"])
                    p.wait("vector", sm["ldh"], tgs)
                    p.op("vector", lambda e: e.tensor_scalar(red[:, 5:6], red[:, 5:6], 0.005, None, ALU.mult))
                    p.drain("vector")
                    p.op("vector", lambda e: e.tensor_scalar(scr[:, 0:512], scr[:, 0:512], red[:, 5:6], None, ALU.mult))
                    p.drain("vector")
                p.wait("vector", sm["st"])
                p.op("vector", lambda e, c2=tc: e.tensor_copy(
                    scr[:, 512 + c2 * 512:1024 + c2 * 512], scr[:, 0:512]), sm["ev"])
                p.drain("vector")
                evd = sm["ev"].n
                p.wait("sync", sm["ev"], evd)
                p.dma("sync", d["sp_yt"][j0:j0 + 128, tc * 512:tc * 512 + 512],
                      scr[:, 512 + tc * 512:1024 + tc * 512], sm["st"])
                p.wait("tensor", sm["ev"], evd)
            p.wait("gpsimd", sm["st"])

    _parts = globals().get("FFN_PARTS", "ab")
    _maxoi = globals().get("FFN_MAXOI", 99)
    _sk = globals().get("FFN_SKIP", "")
    for qt in range(4):
        if qt >= stop_after - 6:
            break
        ffn_quarter(qt, _parts, _maxoi, _sk)
        barrier()

    if stop_after < 11:
        meta = dict(nc=nc, ctx=ctx, prog=p, sems=sm, dram=d, sbuf=sb_, cst=cst)
        return meta
    for tt in range(NT):
        p.wait("sync", sm["st"])
        p.wait("gpsimd", sm["st"])
        tgh = p.dma("sync", ar2f[:, 0:4096], d["sp_y1"][tt * 128:(tt + 1) * 128, :], sm["ldh"])
        for rr in range(4):
            tg = p.dma("gpsimd",
                       ar2f[:, 4096 + rr * 1024: 4096 + (rr + 1) * 1024].rearrange("p (dt f) -> p dt f", f=128),
                       d["sp_yt"][rr * 1024:(rr + 1) * 1024, tt * 128:(tt + 1) * 128]
                       .rearrange("(dt p) f -> p dt f", p=128), sm["ldx"])
        p.wait("tensor", sm["ldx"], tg)
        p.wait("tensor", sm["ldh"], tgh)
        p.wait("tensor", sm["ev"])
        for q8 in range(8):
            for q4 in range(4):
                dtile = q8 * 4 + q4
                p.op("tensor", lambda e, d2=dtile, q2=q4: e.transpose(
                    pE[:, q2 * 128:(q2 + 1) * 128], ar2f[:, 4096 + d2 * 128: 4096 + (d2 + 1) * 128],
                    cst["ident_f"][:]), sem=sm["mm"] if q4 == 3 else None)
            mmq = sm["mm"].n
            p.wait("vector", sm["mm"], mmq)
            sl = ar2f[:, q8 * 512:(q8 + 1) * 512]
            p.op("vector", lambda e, s=sl: e.tensor_add(s, s, pE[:, 0:512]), sm["ev"])
            evq = sm["ev"].n
            p.wait("tensor", sm["ev"], evq)

        def emit_out(tt2=tt):
            p.wait("sync", sm["ev"])
            p.dma("sync", d["out"][tt2 * 128:(tt2 + 1) * 128, :], ar2f[:, 0:4096], sm["st"])
        ln_tile(ar2f[:, 0:4096], HID, emit_out)
        p.wait("gpsimd", sm["ev"])
    p.wait("sync", sm["st"])

    meta = dict(nc=nc, ctx=ctx, prog=p, sems=sm, dram=d, sbuf=sb_, cst=cst)
    return meta


def finalize(meta):
    nc, ctx, p = meta["nc"], meta["ctx"], meta["prog"]
    blk = ctx.enter_context(nc.Block())
    p.run(blk)
    ctx.close()
    return nc


def kernel(**inputs):
    meta = build()
    nc = finalize(meta)
    maps = shard_inputs(inputs)
    res = run_bass_kernel_spmd(nc, maps, core_ids=list(range(NCORES)))
    out = np.zeros((B, N, HID), dtype=np.float32)
    for core in range(NCORES):
        b, j = divmod(core, 4)
        out[b, j * T:(j + 1) * T] = np.asarray(res.results[core]["out"], dtype=np.float32)
    return out


# version 04
# speedup vs baseline: 1.2734x; 1.0419x over previous
import numpy as np
import ml_dtypes
from contextlib import ExitStack

import concourse.bass as bass
import concourse.mybir as mybir
from concourse.bass_utils import run_bass_kernel_spmd

F32 = mybir.dt.float32
BF16 = mybir.dt.bfloat16
I32 = mybir.dt.int32
AF = mybir.ActivationFunctionType
ALU = mybir.AluOpType
AX = mybir.AxisListType
BF = ml_dtypes.bfloat16

B, N, HID = 2, 4096, 4096
H, KVH, Dh, W = 32, 8, 128, 64
GRP = H // KVH
I_FFN, R_LR = 11008, 32
NCORES = 8
T = 1024
TC = T // 512
NT = T // 128
HALO = 64
TH = T + HALO
PFB = 3
DT = HID // 128
IT = I_FFN // 128
NEG = -1.0e30
EPS = 1e-5

ENGINES = ("sync", "gpsimd", "scalar", "vector", "tensor")


class Sem:
    def __init__(self, nc, ctx, name):
        self.h = ctx.enter_context(nc.semaphore(name))
        self.n = 0

    def bump(self, k=1):
        self.n += k
        return self.n


class Prog:

    def __init__(self):
        self.q = {e: [] for e in ENGINES}

    def on(self, eng, fn):
        self.q[eng].append(fn)

    def dma(self, eng, out_ap, in_ap, sem, **kw):
        tgt = sem.bump(16)
        self.on(eng, lambda e, o=out_ap, i=in_ap, k=dict(kw): e.dma_start(out=o, in_=i, **k).then_inc(sem.h, 16))
        return tgt

    def dmat(self, eng, out_ap, in_ap, sem):
        tgt = sem.bump(16)
        self.on(eng, lambda e, o=out_ap, i=in_ap: e.dma_start_transpose(o, i).then_inc(sem.h, 16))
        return tgt

    def wait(self, eng, sem, val=None):
        v = sem.n if val is None else val
        if v > 0:
            self.on(eng, lambda e, s=sem.h, vv=v: e.wait_ge(s, vv))

    def op(self, eng, fn, sem=None, k=1):
        if sem is None:
            self.on(eng, fn)
            return None
        tgt = sem.bump(k)
        self.on(eng, lambda e, f=fn, s=sem.h, kk=k: f(e).then_inc(s, kk))
        return tgt

    def drain(self, eng):
        self.on(eng, lambda e: e.drain())

    def run(self, block):
        for name in ENGINES:
            sec = getattr(block, name)
            ops = self.q[name]

            def make(ops_):
                def body(e):
                    for f in ops_:
                        f(e)
                return body
            sec(make(ops))



def host_consts(core_id):
    b, j = divmod(core_id, 4)
    t0 = j * T
    c = {}
    c["ident_f"] = np.eye(128, dtype=np.float32)
    c["ident_b"] = np.eye(128, dtype=BF)
    jj, ii = np.meshgrid(np.arange(128), np.arange(128), indexing="ij")
    c["tri_incl"] = ((jj <= ii).astype(np.float32) / 16.0)
    c["tri_rev"] = ((jj > ii).astype(np.float32) / 16.0)
    c["mask_tri"] = (jj <= ii).astype(np.float32)
    r = np.arange(128)[:, None]
    col = np.arange(192)[None, :]
    maskA = np.where((col >= r) & (col <= r + 64), 0.0, NEG).astype(np.float32)
    maskB = np.where((col >= r) & (col <= r + 64) & (col >= 64), 0.0, NEG).astype(np.float32)
    c["mask_first"] = maskB if j == 0 else maskA
    c["mask_rest"] = maskA
    pos = np.arange(t0 - HALO, t0 + T, dtype=np.float64)
    half = np.arange(Dh // 2, dtype=np.float64)
    inv = 1.0 / (10000.0 ** (2.0 * half / Dh))
    fr = inv[:, None] * pos[None, :]
    emb = np.concatenate([fr, fr], axis=0)
    c["cos_t"] = np.cos(emb).astype(np.float32)
    c["sin_t"] = np.sin(emb).astype(np.float32)
    c["ones_b"] = np.ones((128, 1), dtype=BF)
    c["neg75"] = np.full((1, 128), -7.5, dtype=BF)
    return c


CONST_SPECS = [
    ("ident_f", [128, 128], F32), ("ident_b", [128, 128], BF16),
    ("tri_incl", [128, 128], F32), ("tri_rev", [128, 128], F32),
    ("mask_tri", [128, 128], F32), ("mask_first", [128, 192], F32),
    ("mask_rest", [128, 192], F32), ("cos_t", [128, TH], F32),
    ("sin_t", [128, TH], F32), ("ones_b", [128, 1], BF16),
    ("neg75", [1, 128], BF16),
]


def shard_inputs(inputs):
    x = inputs["L0_attn_hidden_states"]
    maps = []
    for core in range(NCORES):
        b, j = divmod(core, 4)
        t0 = j * T
        m = {}
        xm = np.zeros((TH, HID), dtype=np.float32)
        lo = max(t0 - HALO, 0)
        xm[HALO - (t0 - lo):] = x[b, lo:t0 + T]
        m["x_main"] = np.ascontiguousarray(xm)
        xp = np.zeros((PFB * T, HID), dtype=np.float32)
        if t0 > 0:
            xp[PFB * T - t0:] = x[b, 0:t0]
        m["x_prefix"] = xp
        for k, v in host_consts(core).items():
            m[k] = np.ascontiguousarray(v)
        for name in ("attn_Wq", "attn_Wk", "attn_Wv", "attn_Wo",
                     "ffn_gate_q", "ffn_gate_scale", "ffn_gate_A", "ffn_gate_B",
                     "ffn_up_q", "ffn_up_scale", "ffn_up_A", "ffn_up_B",
                     "ffn_down_q", "ffn_down_scale", "ffn_down_A", "ffn_down_B"):
            m[name] = inputs["L0_" + name]
        maps.append(m)
    return maps



def build(debug_taps=(), stop_after=99):
    nc = bass.Bass("TRN2", target_bir_lowering=False)
    ctx = ExitStack()
    d = {}
    d["x_main"] = nc.dram_tensor("x_main", [TH, HID], F32, kind="ExternalInput").ap()
    d["x_prefix"] = nc.dram_tensor("x_prefix", [PFB * T, HID], F32, kind="ExternalInput").ap()
    for name, shape, dt in CONST_SPECS:
        d[name] = nc.dram_tensor(name, shape, dt, kind="ExternalInput").ap()
    d["attn_Wq"] = nc.dram_tensor("attn_Wq", [HID, HID], F32, kind="ExternalInput").ap()
    d["attn_Wk"] = nc.dram_tensor("attn_Wk", [HID, KVH * Dh], F32, kind="ExternalInput").ap()
    d["attn_Wv"] = nc.dram_tensor("attn_Wv", [HID, KVH * Dh], F32, kind="ExternalInput").ap()
    d["attn_Wo"] = nc.dram_tensor("attn_Wo", [HID, HID], F32, kind="ExternalInput").ap()
    d["ffn_gate_q"] = nc.dram_tensor("ffn_gate_q", [I_FFN, HID], I32, kind="ExternalInput").ap()
    d["ffn_gate_scale"] = nc.dram_tensor("ffn_gate_scale", [I_FFN], F32, kind="ExternalInput").ap()
    d["ffn_gate_A"] = nc.dram_tensor("ffn_gate_A", [HID, R_LR], F32, kind="ExternalInput").ap()
    d["ffn_gate_B"] = nc.dram_tensor("ffn_gate_B", [R_LR, I_FFN], F32, kind="ExternalInput").ap()
    d["ffn_up_q"] = nc.dram_tensor("ffn_up_q", [I_FFN, HID], I32, kind="ExternalInput").ap()
    d["ffn_up_scale"] = nc.dram_tensor("ffn_up_scale", [I_FFN], F32, kind="ExternalInput").ap()
    d["ffn_up_A"] = nc.dram_tensor("ffn_up_A", [HID, R_LR], F32, kind="ExternalInput").ap()
    d["ffn_up_B"] = nc.dram_tensor("ffn_up_B", [R_LR, I_FFN], F32, kind="ExternalInput").ap()
    d["ffn_down_q"] = nc.dram_tensor("ffn_down_q", [HID, I_FFN], I32, kind="ExternalInput").ap()
    d["ffn_down_scale"] = nc.dram_tensor("ffn_down_scale", [HID], F32, kind="ExternalInput").ap()
    d["ffn_down_A"] = nc.dram_tensor("ffn_down_A", [I_FFN, R_LR], F32, kind="ExternalInput").ap()
    d["ffn_down_B"] = nc.dram_tensor("ffn_down_B", [R_LR, HID], F32, kind="ExternalInput").ap()
    d["out"] = nc.dram_tensor("out", [T, HID], F32, kind="ExternalOutput").ap()
    spill_kind = lambda nm: "ExternalOutput" if nm in debug_taps else "Internal"
    d["sp_y1"] = nc.dram_tensor("sp_y1", [T, HID], F32, kind=spill_kind("sp_y1")).ap()
    d["sp_yt"] = nc.dram_tensor("sp_yt", [HID, T], F32, kind=spill_kind("sp_yt")).ap()

    p = Prog()
    sb_ = {}

    def sb(name, shape, dt, **kw):
        t = ctx.enter_context(nc.sbuf_tensor(name, shape, dt, **kw))
        sb_[name] = t
        return t

    def ps(name, shape, dt=F32):
        t = ctx.enter_context(nc.psum_tensor(name, shape, dt))
        return t

    def dbg(name, ap, shape, dt):
        if name in debug_taps:
            o = nc.dram_tensor("dbg_" + name, shape, dt, kind="ExternalOutput").ap()
            tgt = p.dma("sync", o, ap, sm["st"])
            p.wait("sync", sm["st"], tgt)

    o_T = sb("o_T", [128, H * T], BF16)
    ar2 = sb("ar2", [128, DT * (TH + HALO)], BF16, is_transpose_dest=True)
    qTh = sb("qTh", [128, T], BF16)
    kTh = sb("kTh", [128, TH], BF16)
    skh = sb("skh", [128, TH], BF16)
    v_nt = sb("v_nt", [128, (NT + 1) * 128], BF16, is_transpose_dest=True)
    v_gla = sb("v_gla", [128, NT * 128], BF16, is_transpose_dest=True)
    vTst = sb("vTst", [128, TH + HALO], BF16)
    ks_h = sb("ks_h", [128, T], BF16)
    gs_h = sb("gs_h", [128, T], F32)
    S_f = sb("S_f", [128, KVH * Dh], F32)
    S_b = sb("S_b", [128, KVH * Dh], BF16)
    wst = sb("wst", [128, DT * 128], BF16)
    wst2 = sb("wst2", [128, DT * 128], BF16, is_transpose_dest=True)
    scr = sb("scr", [128, 1536], F32)
    scrb = sb("scrb", [128, 1024], BF16)
    red = sb("red", [128, 16], F32)
    cst = {}
    for name, shape, dt in CONST_SPECS:
        cst[name] = sb("c_" + name, shape, dt)

    sm = {k: Sem(nc, ctx, k) for k in
          ("ldw", "ldx", "ldh", "lw1", "lw2", "lda", "mm", "ev", "sc", "tr", "st", "ph", "pf")}

    def barrier():
        tgt = sm["ph"].n + 5
        for e in ENGINES:
            p.on(e, lambda en, s=sm["ph"].h: en.sem_inc(s, 1))
        sm["ph"].n = tgt
        for e in ENGINES:
            p.wait(e, sm["ph"], tgt)

    for name, shape, dt in CONST_SPECS:
        p.dma("sync", cst[name][:], d[name][:], sm["ldh"])
    for e in ENGINES:
        p.wait(e, sm["ldh"])
    p.op("vector", lambda e: e.memset(S_f[:], 0.0), sm["ev"])
    p.op("vector", lambda e: e.memset(red[:, 9:10], EPS), sm["ev"])
    p.op("vector", lambda e: e.memset(S_b[:], 0.0), sm["ev"])
    p.op("vector", lambda e: e.memset(vTst[:], 0.0), sm["ev"])
    barrier()

    def build_xT(dst, dst_stride, src_ap, rows, tag):
        ntile = (rows + 127) // 128
        for tt in range(ntile):
            r0 = tt * 128
            nr = min(128, rows - r0)
            stg = scrb
            p.dma("gpsimd", wst[0:nr, 0:4096], src_ap[r0:r0 + nr, :], sm["lw1"])
            p.wait("sync", sm["lw1"])
            if nr < 128:
                p.wait("vector", sm["lw1"])
                p.op("vector", lambda e, a=wst[nr:128, 0:4096]: e.memset(a, 0.0), sm["ev"])
                p.wait("sync", sm["ev"])
            src3 = wst[:, 0:4096].rearrange("p (dd f) -> p dd f", f=128)
            dst3 = dst[:, 0:DT * dst_stride].rearrange("p (dd t) -> p dd t", t=dst_stride)[:, :, r0:r0 + 128]
            tgt = p.dmat("sync", dst3, src3, sm["tr"])
            p.wait("gpsimd", sm["tr"], tgt)
        p.wait("tensor", sm["tr"])
        p.wait("vector", sm["tr"])

    kt_all = sb("kt_all", [128, NT * 128], BF16)
    ea_all = sb("ea_all", [128, NT * 128], F32)
    S_snap = sb("S_snap", [128, (NT + 1) * 128], BF16)
    sqh = sb("sqh", [128, T], BF16)

    pA = ps("pA", [128, 512])
    pB = ps("pB", [128, 512])
    pC = ps("pC", [128, 512])
    pD = ps("pD", [128, 512])
    pE = ps("pE", [128, 512])
    pF = ps("pF", [128, 1024], BF16)
    pG = ps("pG", [128, 512])
    pH = ps("pH", [128, 1024], BF16)

    ISQ = 1.0 / float(np.sqrt(128.0))
    XTS = TH + HALO

    def load_wstrip(dst, wd, col0, width=128, rows=HID, after=None):
        if after is None:
            p.wait("gpsimd", sm["mm"])
        elif after:
            p.wait("gpsimd", sm[after[0]], after[1])
        sem = sm["lw1"] if dst is wst else sm["lw2"]
        tg = None
        rchunk = 1024
        for r0 in range(0, rows, rchunk):
            src = wd[r0:r0 + rchunk, col0:col0 + width].rearrange("(dd p) o -> p dd o", p=128)
            c0 = (r0 // 128) * width
            dstc = dst[:, c0:c0 + (rchunk // 128) * width].rearrange("p (dd o) -> p dd o", o=width)
            tg = p.dma("gpsimd", dstc, src, sem)
        return tg

    def proj_strip(wdst, xts, rhs_base_off, ntok, out_cb, ld_tgt):
        p.wait("tensor", sm["lw1"] if wdst is wst else sm["lw2"], ld_tgt)
        nch = (ntok + 511) // 512
        sc_done = []
        for c in range(nch):
            t0c = c * 512
            n = min(512, ntok - t0c)
            pb_ = pA if c % 2 == 0 else pB
            if c >= 2:
                p.wait("tensor", sm["sc"], sc_done[c - 2])
            for dd in range(DT):
                lhs_ap = wdst[:, dd * 128:(dd + 1) * 128]
                rhs_ap = ar2[:, dd * xts + rhs_base_off + t0c: dd * xts + rhs_base_off + t0c + n]
                p.op("tensor",
                     lambda e, pb2=pb_, L=lhs_ap, R=rhs_ap, s=(dd == 0), st=(dd == DT - 1), n2=n:
                     e.matmul(pb2[:, 0:n2], L, R, start=s, stop=st),
                     sem=sm["mm"] if dd == DT - 1 else None)
            out_cb(pb_, c, t0c, n)
            sc_done.append(sm["sc"].n)

    def rope_emit(dst, srcF, c0, ncols, tab0):
        co, si = cst["cos_t"], cst["sin_t"]
        sl = slice(c0, c0 + ncols)
        tl = slice(tab0, tab0 + ncols)
        seq = [
            lambda e: e.tensor_copy(dst[0:64, sl], srcF[64:128, sl]),
            lambda e: e.tensor_mul(scr[0:64, 0:ncols], dst[0:64, sl], si[0:64, tl]),
            lambda e: e.tensor_mul(dst[0:64, sl], srcF[0:64, sl], co[0:64, tl]),
            lambda e: e.tensor_sub(dst[0:64, sl], dst[0:64, sl], scr[0:64, 0:ncols]),
            lambda e: e.tensor_copy(dst[64:128, sl], srcF[0:64, sl]),
            lambda e: e.tensor_mul(scr[64:128, 0:ncols], dst[64:128, sl], si[64:128, tl]),
            lambda e: e.tensor_mul(dst[64:128, sl], srcF[64:128, sl], co[64:128, tl]),
            lambda e: e.tensor_add(dst[64:128, sl], dst[64:128, sl], scr[64:128, 0:ncols]),
        ]
        for ii, f in enumerate(seq):
            if ii == len(seq) - 1:
                p.op("vector", f, sm["ev"])
            else:
                p.op("vector", f)
                p.drain("vector")

    bufmm = {"wst": 0, "wst2": 0}

    def kv_pipeline(kvh, prefix, koff, xts):
        ntok = koff + T
        lt = load_wstrip(wst, d["attn_Wk"], kvh * 128, after=("mm", bufmm["wst"]))

        def ev_k(pb_, c, rel0, n):
            tgt = sm["mm"].n
            p.wait("scalar", sm["mm"], tgt)
            p.op("scalar", lambda e, pb2=pb_, r2=rel0, n2=n:
                 e.copy(kTh[:, r2:r2 + n2], pb2[:, 0:n2]), sm["sc"])
        proj_strip(wst, xts, 0, ntok, ev_k, lt)
        bufmm["wst"] = sm["mm"].n
        if not prefix and kvh == KVH - 1:
            p.wait("sync", sm["sc"])
            dbg("kTh_early", kTh[:], [128, TH], BF16)
        lt = load_wstrip(wst2, d["attn_Wv"], kvh * 128, after=("mm", bufmm["wst2"]))
        p.wait("tensor", sm["sc"])

        def ev_v(pb_, c, rel0, n):
            tgt = sm["mm"].n
            p.wait("scalar", sm["mm"], tgt)
            p.op("scalar", lambda e, pb2=pb_, r2=rel0, n2=n:
                 e.copy(vTst[:, r2:r2 + n2], pb2[:, 0:n2]), sm["sc"])
        proj_strip(wst2, xts, 0, ntok, ev_v, lt)
        bufmm["wst2"] = sm["mm"].n
        p.wait("sync", sm["sc"])
        src3 = vTst[:, koff:koff + NT * 128].rearrange("p (g f) -> p g f", f=128)
        dst3 = v_gla[:].rearrange("p (g f) -> p g f", f=128)
        tg_vg = p.dmat("sync", dst3, src3, sm["tr"])
        if not prefix:
            src3 = vTst[:, 0:(NT + 1) * 128].rearrange("p (g f) -> p g f", f=128)
            dst3 = v_nt[:].rearrange("p (g f) -> p g f", f=128)
            tg_vg = p.dmat("sync", dst3, src3, sm["tr"])
            p.wait("vector", sm["sc"])
            for cc in range(0, ntok, 512):
                nn = min(512, ntok - cc)
                rope_emit(skh, kTh, cc, nn, cc)
        p.wait("tensor", sm["sc"])
        for tt in range(NT):
            p.op("tensor", lambda e, t2=tt: e.transpose(
                pF[:, 0:128], kTh[:, koff + t2 * 128: koff + (t2 + 1) * 128], cst["ident_b"][:]), sm["mm"])
            p.wait("scalar", sm["mm"])
            p.op("scalar", lambda e, t2=tt: e.copy(gs_h[:, t2 * 128:(t2 + 1) * 128], pF[:, 0:128]), sm["sc"])
            p.wait("tensor", sm["sc"])
        p.drain("scalar")
        p.wait("vector", sm["sc"])
        for tt in range(NT):
            sl = slice(tt * 128, (tt + 1) * 128)
            p.op("vector", lambda e, s=sl: e.reduce_max(red[:, 0:1], gs_h[:, s], axis=AX.X, negate=True), sm["ev"])
            p.wait("scalar", sm["ev"])
            p.op("scalar", lambda e, s=sl: e.activation(
                scr[:, 0:128], gs_h[:, s], AF.Exp, bias=red[:, 0:1], accum_out=red[:, 1:2]), sm["sc"])
            p.wait("vector", sm["sc"])
            p.op("vector", lambda e: e.reciprocal(red[:, 2:3], red[:, 1:2]))
            p.drain("vector")
            p.op("vector", lambda e, s=sl: e.tensor_scalar(
                ks_h[:, s], scr[:, 0:128], red[:, 2:3], None, ALU.mult), sm["ev"])
        p.wait("scalar", sm["ev"])
        p.op("scalar", lambda e: e.activation(scr[:, 0:1024], gs_h[:, 0:1024], AF.Sigmoid), sm["sc"])
        p.drain("scalar")
        p.op("scalar", lambda e: e.activation(gs_h[:, 0:1024], scr[:, 0:1024], AF.Ln), sm["sc"])
        if not prefix and kvh == KVH - 1:
            p.wait("sync", sm["sc"])
            p.wait("sync", sm["ev"])
            dbg("kTh_mid", kTh[:], [128, TH], BF16)
        p.wait("tensor", sm["ev"])
        p.wait("tensor", sm["sc"])
        p.wait("tensor", sm["tr"], tg_vg)
        Scur = S_f[:, kvh * 128:(kvh + 1) * 128]
        if not prefix:
            p.op("vector", lambda e: e.tensor_copy(S_snap[:, 0:128], Scur), sm["ev"])
        for tt in range(NT):
            gsl = gs_h[:, tt * 128:(tt + 1) * 128]
            ksl = ks_h[:, tt * 128:(tt + 1) * 128]
            p.op("tensor", lambda e, g=gsl: e.matmul(pC[:, 0:128], cst["tri_rev"][:], g, start=True, stop=True))
            p.op("tensor", lambda e, g=gsl: e.matmul(pD[:, 0:128], g, cst["tri_incl"][:], start=True, stop=True))
            p.op("tensor", lambda e, g=gsl: e.matmul(pC[:, 128:256], cst["tri_incl"][:], g, start=True, stop=True), sm["mm"])
            mm0 = sm["mm"].n
            p.wait("scalar", sm["mm"], mm0)
            p.op("scalar", lambda e: e.activation(scr[:, 0:128], pC[:, 0:128], AF.Exp))
            p.drain("scalar")
            p.op("scalar", lambda e, t2=tt: e.activation(ea_all[:, t2 * 128:(t2 + 1) * 128], pC[:, 128:256], AF.Exp))
            p.drain("scalar")
            p.op("scalar", lambda e: e.activation(scr[:, 128:256], pC[:, 128:256], AF.Exp, scale=-1.0))
            p.drain("scalar")
            p.op("scalar", lambda e: e.activation(red[:, 4:5], pD[:, 127:128], AF.Exp), sm["sc"])
            sc0 = sm["sc"].n
            p.wait("vector", sm["sc"], sc0)
            p.op("vector", lambda e, k=ksl: e.tensor_mul(scrb[:, 0:128], k, scr[:, 0:128]))
            p.drain("vector")
            p.op("vector", lambda e, k=ksl: e.tensor_mul(scrb[:, 128:256], k, scr[:, 128:256]), sm["ev"])
            ev0 = sm["ev"].n
            p.wait("tensor", sm["ev"], ev0)
            p.op("tensor", lambda e: e.transpose(pF[:, 0:128], scrb[:, 128:256], cst["ident_b"][:]), sm["mm"])
            p.wait("vector", sm["mm"])
            p.op("vector", lambda e, t2=tt: e.tensor_copy(kt_all[:, t2 * 128:(t2 + 1) * 128], pF[:, 0:128]), sm["ev"])
            p.op("tensor", lambda e, t2=tt: e.matmul(
                pE[:, 0:128], scrb[:, 0:128], v_gla[:, t2 * 128:(t2 + 1) * 128], start=True, stop=True), sm["mm"])
            mm1 = sm["mm"].n
            p.wait("vector", sm["mm"], mm1)
            p.op("vector", lambda e, S=Scur: e.scalar_tensor_tensor(
                S, S, red[:, 4:5], pE[:, 0:128], ALU.mult, ALU.add))
            p.drain("vector")
            if not prefix:
                p.op("vector", lambda e, t2=tt, S=Scur: e.tensor_copy(
                    S_snap[:, (t2 + 1) * 128:(t2 + 2) * 128], S), sm["ev"])
            else:
                p.op("vector", lambda e: e.engine_nop() if False else e.tensor_copy(red[:, 8:9], red[:, 4:5]), sm["ev"])
            evS = sm["ev"].n
            p.wait("tensor", sm["ev"], evS)
        if prefix:
            return
        for qh in range(GRP):
            hq = kvh * GRP + qh
            lt = load_wstrip(wst, d["attn_Wq"], hq * 128, after=("mm", bufmm["wst"]))

            def ev_q(pb_, c, rel0, n):
                tgt = sm["mm"].n
                p.wait("scalar", sm["mm"], tgt)
                p.op("scalar", lambda e, pb2=pb_, r2=rel0, n2=n:
                     e.copy(qTh[:, r2:r2 + n2], pb2[:, 0:n2]), sm["sc"])
            proj_strip(wst, xts, HALO, T, ev_q, lt)
            bufmm["wst"] = sm["mm"].n
            p.wait("vector", sm["sc"])
            for cc in range(0, T, 512):
                nn = min(512, T - cc)
                rope_emit(sqh, qTh, cc, nn, HALO + cc)
            p.wait("tensor", sm["ev"])
            for tt in range(NT):
                p.op("tensor", lambda e, t2=tt: e.transpose(
                    pF[:, 0:128], qTh[:, t2 * 128:(t2 + 1) * 128], cst["ident_b"][:]), sm["mm"])
                mmq = sm["mm"].n
                p.wait("vector", sm["mm"], mmq)
                p.op("vector", lambda e: e.reduce_max(red[:, 0:1], pF[:, 0:128], axis=AX.X, negate=True), sm["ev"])
                p.wait("scalar", sm["ev"])
                p.op("scalar", lambda e: e.activation(
                    scr[:, 0:128], pF[:, 0:128], AF.Exp, bias=red[:, 0:1], accum_out=red[:, 1:2]), sm["sc"])
                p.wait("vector", sm["sc"])
                p.op("vector", lambda e: e.reciprocal(red[:, 2:3], red[:, 1:2]))
                p.drain("vector")
                p.op("vector", lambda e: e.tensor_scalar(
                    scr[:, 128:256], scr[:, 0:128], red[:, 2:3], 0.5, ALU.mult, ALU.mult))
                p.drain("vector")
                p.op("vector", lambda e, t2=tt: e.tensor_mul(
                    scrb[:, 0:128], scr[:, 128:256], ea_all[:, t2 * 128:(t2 + 1) * 128]), sm["ev"])
                ev1 = sm["ev"].n
                p.wait("tensor", sm["ev"], ev1)
                p.op("tensor", lambda e: e.transpose(pH[:, 0:128], scrb[:, 0:128], cst["ident_b"][:]), sm["mm"])
                p.wait("vector", sm["mm"])
                p.op("vector", lambda e: e.tensor_copy(scrb[:, 256:384], pH[:, 0:128]), sm["ev"])
                ev2 = sm["ev"].n
                p.wait("tensor", sm["ev"], ev2)
                p.op("tensor", lambda e, t2=tt: e.matmul(
                    pE[:, 0:128], kt_all[:, t2 * 128:(t2 + 1) * 128], scrb[:, 256:384],
                    start=True, stop=True), sm["mm"])
                p.wait("vector", sm["mm"])
                p.op("vector", lambda e: e.tensor_mul(scrb[:, 384:512], pE[:, 0:128], cst["mask_tri"][:]), sm["ev"])
                ev3 = sm["ev"].n
                p.op("tensor", lambda e, t2=tt: e.matmul(
                    pA[:, 0:192], sqh[:, t2 * 128:(t2 + 1) * 128], skh[:, t2 * 128:t2 * 128 + 192],
                    start=True, stop=True), sm["mm"])
                mmS = sm["mm"].n
                p.wait("vector", sm["mm"], mmS)
                mtile = cst["mask_first"] if tt == 0 else cst["mask_rest"]
                p.op("vector", lambda e, m=mtile: e.scalar_tensor_tensor(
                    scr[:, 256:448], pA[:, 0:192], ISQ, m[:], ALU.mult, ALU.add))
                p.drain("vector")
                p.op("vector", lambda e: e.reduce_max(red[:, 0:1], scr[:, 256:448], axis=AX.X, negate=True), sm["ev"])
                p.wait("scalar", sm["ev"])
                p.op("scalar", lambda e: e.activation(
                    scr[:, 448:640], scr[:, 256:448], AF.Exp, bias=red[:, 0:1], accum_out=red[:, 1:2]), sm["sc"])
                p.wait("vector", sm["sc"])
                p.op("vector", lambda e: e.reciprocal(red[:, 2:3], red[:, 1:2]))
                p.drain("vector")
                p.op("vector", lambda e: e.tensor_scalar(
                    scrb[:, 512:704], scr[:, 448:640], red[:, 2:3], 0.5, ALU.mult, ALU.mult), sm["ev"])
                ev4 = sm["ev"].n
                p.wait("tensor", sm["ev"], ev4)
                p.op("tensor", lambda e: e.transpose(pH[:, 128:256], scrb[:, 512:640], cst["ident_b"][:]))
                p.op("tensor", lambda e: e.transpose(pH[0:64, 256:384], scrb[:, 640:704], cst["ident_b"][:]), sm["mm"])
                mmT = sm["mm"].n
                p.wait("vector", sm["mm"], mmT)
                p.op("vector", lambda e: e.tensor_copy(scrb[:, 704:832], pH[:, 128:256]))
                p.drain("vector")
                p.op("vector", lambda e: e.tensor_copy(scrb[0:64, 832:960], pH[0:64, 256:384]), sm["ev"])
                ev5 = sm["ev"].n
                p.wait("tensor", sm["ev"], ev5)
                p.op("tensor", lambda e, t2=tt: e.matmul(
                    pG[:, 0:128], S_snap[:, t2 * 128:(t2 + 1) * 128], scrb[:, 256:384],
                    start=True, stop=False))
                p.op("tensor", lambda e, t2=tt: e.matmul(
                    pG[:, 0:128], v_gla[:, t2 * 128:(t2 + 1) * 128], scrb[:, 384:512],
                    start=False, stop=False))
                p.op("tensor", lambda e, t2=tt: e.matmul(
                    pG[:, 0:128], v_nt[:, t2 * 128:(t2 + 1) * 128], scrb[:, 704:832],
                    start=False, stop=False))
                p.op("tensor", lambda e, t2=tt: e.matmul(
                    pG[:, 0:128], v_nt[0:64, (t2 + 1) * 128:(t2 + 2) * 128], scrb[0:64, 832:960],
                    start=False, stop=True), sm["mm"])
                mmO = sm["mm"].n
                p.wait("scalar", sm["mm"], mmO)
                p.op("scalar", lambda e, t2=tt, h2=hq: e.copy(
                    o_T[:, h2 * T + t2 * 128: h2 * T + (t2 + 1) * 128], pG[:, 0:128]), sm["sc"])
                p.wait("tensor", sm["sc"])

    for pb in range(PFB):
        p.op("vector", lambda e: e.memset(ar2[:, 0:DT * 1024], 0.0), sm["ev"])
        p.wait("sync", sm["ev"])
        p.wait("gpsimd", sm["ev"])
        build_xT(ar2, 1024, d["x_prefix"][pb * T:(pb + 1) * T, :], T, f"xp{pb}")
        barrier()
        for kvh in range(KVH):
            kv_pipeline(kvh, prefix=True, koff=0, xts=1024)
            barrier()

    p.op("vector", lambda e: e.memset(ar2[:], 0.0), sm["ev"])
    p.wait("sync", sm["ev"])
    p.wait("gpsimd", sm["ev"])
    build_xT(ar2, XTS, d["x_main"], TH, "xmain")
    barrier()
    dbg("xT", ar2[:], [128, DT * (TH + HALO)], BF16)

    for kvh in range(KVH):
        kv_pipeline(kvh, prefix=False, koff=HALO, xts=XTS)
        barrier()
    dbg("o_T", o_T[:], [128, H * T], BF16)
    dbg("S_f", S_f[:], [128, KVH * 128], F32)
    dbg("kTh", kTh[:], [128, TH], BF16)
    dbg("skh", skh[:], [128, TH], BF16)
    dbg("ks_h", ks_h[:], [128, T], BF16)
    dbg("gs_h", gs_h[:], [128, T], F32)
    dbg("ea_all", ea_all[:], [128, NT * 128], F32)
    dbg("kt_all", kt_all[:], [128, NT * 128], BF16)
    dbg("S_snap", S_snap[:], [128, (NT + 1) * 128], BF16)
    dbg("sqh", sqh[:], [128, T], BF16)
    dbg("qTh", qTh[:], [128, T], BF16)
    dbg("v_nt", v_nt[:], [128, (NT + 1) * 128], BF16)
    dbg("v_gla", v_gla[:], [128, NT * 128], BF16)


    if stop_after < 5:
        meta = dict(nc=nc, ctx=ctx, prog=p, sems=sm, dram=d, sbuf=sb_, cst=cst)
        return meta
    ar2f = ar2[:].bitcast(F32)
    Y1S = 4096

    def ln_tile(buf_ap, ncols, out_cb):
        p.op("vector", lambda e: e.reduce_sum(red[:, 0:1], buf_ap, axis=AX.X))
        p.drain("vector")
        p.op("vector", lambda e: e.tensor_scalar(red[:, 0:1], red[:, 0:1], 1.0 / ncols, None, ALU.mult), sm["ev"])
        p.wait("scalar", sm["ev"])
        wf0, wf1 = wst[:].bitcast(F32), wst2[:].bitcast(F32)
        half = ncols // 2
        p.wait("scalar", sm["tr"])
        p.op("scalar", lambda e: e.activation(wf0[:, 0:half], buf_ap[:, 0:half], AF.Square,
                                              accum_out=red[:, 1:2]))
        p.drain("scalar")
        p.op("scalar", lambda e: e.activation(wf1[:, 0:half], buf_ap[:, half:ncols], AF.Square,
                                              accum_out=red[:, 4:5]), sm["sc"])
        p.wait("vector", sm["sc"])
        p.op("vector", lambda e: e.tensor_add(red[:, 1:2], red[:, 1:2], red[:, 4:5]))
        p.drain("vector")
        p.op("vector", lambda e: e.tensor_scalar(red[:, 1:2], red[:, 1:2], 1.0 / ncols, None, ALU.mult))
        p.drain("vector")
        p.op("vector", lambda e: e.tensor_mul(red[:, 4:5], red[:, 0:1], red[:, 0:1]))
        p.drain("vector")
        p.op("vector", lambda e: e.tensor_sub(red[:, 1:2], red[:, 1:2], red[:, 4:5]), sm["ev"])
        p.wait("scalar", sm["ev"])
        p.op("scalar", lambda e: e.activation(red[:, 2:3], red[:, 1:2], AF.Ln, bias=red[:, 9:10]))
        p.drain("scalar")
        p.op("scalar", lambda e: e.activation(red[:, 2:3], red[:, 2:3], AF.Exp, scale=-0.5), sm["sc"])
        p.wait("vector", sm["sc"])
        p.op("vector", lambda e: e.tensor_scalar(buf_ap, buf_ap, red[:, 0:1], red[:, 2:3],
                                                 ALU.subtract, ALU.mult), sm["ev"])
        p.drain("vector")
        out_cb()

    for tc in range(TC):
        p.wait("vector", sm["st"])
        buf_busy = {0: ("mm", 0), 1: ("mm", 0)}
        for jt in range(DT):
            wbuf = wst if jt % 2 == 0 else wst2
            lt = load_wstrip(wbuf, d["attn_Wo"], jt * 128, after=buf_busy[jt % 2])
            p.wait("tensor", sm["lw1"] if jt % 2 == 0 else sm["lw2"], lt)
            pb_ = pA if jt % 2 == 0 else pB
            for ot in range(DT):
                lhs_ap = wbuf[:, ot * 128:(ot + 1) * 128]
                rhs_ap = o_T[:, ot * T + tc * 512: ot * T + tc * 512 + 512]
                p.op("tensor", lambda e, pb2=pb_, L=lhs_ap, R=rhs_ap,
                     s=(ot == 0), st=(ot == DT - 1):
                     e.matmul(pb2[:, 0:512], L, R, start=s, stop=st),
                     sem=sm["mm"] if ot == DT - 1 else None)
            buf_busy[jt % 2] = ("mm", sm["mm"].n)
            p.wait("vector", sm["mm"])
            slot = (jt % 2) * 512
            p.op("vector", lambda e, pb2=pb_, s0=slot: e.tensor_copy(scr[:, s0:s0 + 512], pb2[:, 0:512]), sm["ev"])
            ev0 = sm["ev"].n
            p.wait("tensor", sm["ev"], ev0)
            for q4 in range(4):
                p.op("tensor", lambda e, s0=slot, q2=q4: e.transpose(
                    pE[:, q2 * 128:(q2 + 1) * 128], scr[:, s0 + q2 * 128: s0 + (q2 + 1) * 128],
                    cst["ident_f"][:]), sem=sm["mm"] if q4 == 3 else None)
            mmT = sm["mm"].n
            p.wait("vector", sm["mm"], mmT)
            for q4 in range(4):
                dst = ar2f[:, q4 * Y1S + jt * 128: q4 * Y1S + (jt + 1) * 128]
                p.op("vector", lambda e, dd=dst, q2=q4: e.tensor_copy(dd, pE[:, q2 * 128:(q2 + 1) * 128]),
                     sem=sm["ev"] if q4 == 3 else None)
            evT = sm["ev"].n
            p.wait("tensor", sm["ev"], evT)
        p.wait("gpsimd", sm["ev"])
        p.wait("gpsimd", sm["mm"])
        for q4 in range(4):
            tglob = tc * 512 + q4 * 128
            wfs = [wst[:].bitcast(F32), wst2[:].bitcast(F32)]
            p.dma("gpsimd", wfs[0][:, 0:2048],
                  d["x_main"][HALO + tglob: HALO + tglob + 128, 0:2048], sm["lw1"])
            p.dma("gpsimd", wfs[1][:, 0:2048],
                  d["x_main"][HALO + tglob: HALO + tglob + 128, 2048:4096], sm["lw2"])
            p.wait("vector", sm["lw1"])
            p.wait("vector", sm["lw2"])
            for hh in range(2):
                sl = ar2f[:, q4 * Y1S + hh * 2048: q4 * Y1S + (hh + 1) * 2048]
                p.op("vector", lambda e, s=sl, h2=hh: e.tensor_add(s, s, wfs[h2][:, 0:2048]),
                     sem=sm["ev"] if hh == 1 else None)
            p.drain("vector")
            p.wait("gpsimd", sm["ev"])

            def emit_y1(q2=q4, tg2=tglob):
                p.wait("sync", sm["ev"])
                p.dma("sync", d["sp_y1"][tg2:tg2 + 128, :], ar2f[:, q2 * Y1S:(q2 + 1) * Y1S], sm["st"])
                p.wait("gpsimd", sm["ev"])
                p.op("vector", lambda e: e.tensor_copy(wst[:, 0:4096], ar2f[:, q2 * Y1S:(q2 + 1) * Y1S]), sm["ev"])
                evc = sm["ev"].n
                p.wait("sync", sm["ev"], evc)
                src3 = wst[:, 0:4096].rearrange("p (dd f) -> p dd f", f=128)
                dst3 = o_T[:].rearrange("p (dd t) -> p dd t", t=T)[:, :, tg2:tg2 + 128]
                tgt = p.dmat("sync", dst3, src3, sm["tr"])
                p.wait("gpsimd", sm["tr"], tgt)
                p.wait("vector", sm["tr"], tgt)
            ln_tile(ar2f[:, q4 * Y1S:(q4 + 1) * Y1S], HID, emit_y1)
    barrier()
    dbg("y1T", o_T[:], [128, DT * T], BF16)


    if stop_after < 6:
        meta = dict(nc=nc, ctx=ctx, prog=p, sems=sm, dram=d, sbuf=sb_, cst=cst)
        return meta
    y1T = o_T
    p.wait("tensor", sm["tr"])
    p.wait("vector", sm["tr"])
    p.wait("scalar", sm["tr"])
    BSG, BSU, BSD = 22528, 28032, 33536

    def bs_ap(off, oi):
        pb0 = 64 * (oi % 2)
        c0 = off + (oi // 2) * 128
        return ar2[pb0:pb0 + 33, c0:c0 + 128]

    for tc in range(TC):
        for dd in range(DT):
            p.op("tensor", lambda e, d2=dd, c2=tc: e.matmul(
                pA[0:1, 0:512], cst["ones_b"][:], y1T[:, d2 * T + c2 * 512: d2 * T + c2 * 512 + 512],
                start=(d2 == 0), stop=(d2 == DT - 1)), sem=sm["mm"] if dd == DT - 1 else None)
        p.wait("scalar", sm["mm"])
        p.op("scalar", lambda e, c2=tc: e.copy(skh[0:1, c2 * 512: c2 * 512 + 512], pA[0:1, 0:512]), sm["sc"])
        p.wait("vector", sm["sc"])
        for rdst in (scrb, qTh):
            for rr in (32, 96):
                p.op("vector", lambda e, c2=tc, rd=rdst, r2=rr: e.tensor_copy(
                    rd[r2:r2 + 1, c2 * 512:c2 * 512 + 512], skh[0:1, c2 * 512:c2 * 512 + 512]))
                p.drain("vector")
        p.op("vector", lambda e: e.tensor_copy(red[:, 10:11], red[:, 9:10]), sm["ev"])
        p.wait("tensor", sm["sc"])

    for which, wdram, rows0 in (("g", d["ffn_gate_A"], 0), ("u", d["ffn_up_A"], 32)):
        p.wait("vector", sm["sc"])
        lt = load_wstrip(wst, wdram, 0, width=32, after=("mm", bufmm["wst"]))
        p.wait("tensor", sm["lw1"], lt)
        tbuf = scrb if which == "g" else qTh
        bufmm_t1 = True
        for tc in range(TC):
            for dd in range(DT):
                p.op("tensor", lambda e, d2=dd, c2=tc: e.matmul(
                    pA[0:32, 0:512], wst[:, d2 * 32:(d2 + 1) * 32],
                    y1T[:, d2 * T + c2 * 512: d2 * T + c2 * 512 + 512],
                    start=(d2 == 0), stop=(d2 == DT - 1)), sem=sm["mm"] if dd == DT - 1 else None)
            bufmm["wst"] = sm["mm"].n
            p.wait("vector", sm["mm"])
            p.op("vector", lambda e, c2=tc, tb=tbuf: e.tensor_scalar(
                tb[0:32, c2 * 512:c2 * 512 + 512], pA[0:32, 0:512], 0.02, None, ALU.mult))
            p.drain("vector")
            p.op("vector", lambda e, c2=tc, tb=tbuf: e.tensor_scalar(
                tb[64:96, c2 * 512:c2 * 512 + 512], pA[0:32, 0:512], 0.02, None, ALU.mult), sm["ev"])
            p.wait("tensor", sm["ev"])

    def build_bs(bdram, sdram, n_tiles, dst_off):
        for oi in range(n_tiles):
            tg = p.dma("sync", scr[0:32, 0:128], bdram[0:32, oi * 128:(oi + 1) * 128], sm["ldh"])
            tg2 = p.dma("sync", red[:, 5:6],
                        sdram[oi * 128:(oi + 1) * 128].rearrange("(p o) -> p o", o=1), sm["ldh"])
            p.wait("vector", sm["ldh"], tg2)
            p.op("vector", lambda e: e.tensor_scalar(red[:, 6:7], red[:, 5:6], 0.005, None, ALU.mult))
            p.drain("vector")
            p.op("vector", lambda e: e.reciprocal(red[:, 7:8], red[:, 6:7]), sm["ev"])
            p.drain("vector")
            ev0 = sm["ev"].n
            p.wait("tensor", sm["ldh"])
            p.op("tensor", lambda e: e.transpose(pE[:, 0:32], scr[0:32, 0:128], cst["ident_f"][0:32, 0:32]), sm["mm"])
            mm0 = sm["mm"].n
            p.wait("vector", sm["mm"], mm0)
            p.op("vector", lambda e: e.tensor_scalar(vTst[:, 0:32], pE[:, 0:32], red[:, 7:8], None, ALU.mult), sm["ev"])
            ev1 = sm["ev"].n
            p.wait("tensor", sm["ev"], ev1)
            p.op("tensor", lambda e: e.transpose(pF[0:32, 0:128], vTst[:, 0:32], cst["ident_b"][:]), sm["mm"])
            mm1 = sm["mm"].n
            p.wait("scalar", sm["mm"], mm1)
            p.op("scalar", lambda e, o2=oi: e.copy(
                ar2[64 * (o2 % 2):64 * (o2 % 2) + 32, dst_off + (o2 // 2) * 128: dst_off + (o2 // 2) * 128 + 128],
                pF[0:32, 0:128]), sm["sc"])
            p.wait("tensor", sm["sc"])
            p.wait("sync", sm["sc"])
    p.wait("vector", sm["st"])
    p.wait("scalar", sm["st"])
    p.op("vector", lambda e: e.memset(ar2[32:33, BSG:BSG + 13056], -7.5))
    p.drain("vector")
    p.op("vector", lambda e: e.memset(ar2[96:97, BSG:BSG + 13056], -7.5), sm["ev"])
    p.wait("sync", sm["ev"])
    p.wait("scalar", sm["ev"])
    build_bs(d["ffn_gate_B"], d["ffn_gate_scale"], IT, BSG)
    build_bs(d["ffn_up_B"], d["ffn_up_scale"], IT, BSU)
    build_bs(d["ffn_down_B"], d["ffn_down_scale"], DT, BSD)
    p.op("vector", lambda e: e.memset(kTh[:, 32:33], 1.0), sm["ev"])
    barrier()

    if stop_after < 6.5:
        meta = dict(nc=nc, ctx=ctx, prog=p, sems=sm, dram=d, sbuf=sb_, cst=cst)
        return meta
    QW = [22, 22, 21, 21]
    QB = [0, 22, 44, 65]
    t3_open = [False]

    def ffn_quarter(qt, parts="ab", maxoi=99, skips=""):
        qw, qb = QW[qt], QB[qt]
        p.wait("vector", sm["st"])
        p.wait("scalar", sm["st"])
        wst_free = sm["tr"].n
        kth_free = sm["mm"].n
        for ol in range(min(qw, maxoi) if "a" in parts else 0):
            oi = qb + ol
            p.wait("gpsimd", sm["tr"], wst_free)
            ltg = p.dma("gpsimd", wst[:, 0:4096], d["ffn_gate_q"][oi * 128:(oi + 1) * 128, :], sm["lw1"])
            p.wait("sync", sm["lw1"], ltg)
            p.wait("sync", sm["mm"])
            src3 = wst[:, 0:4096].rearrange("p (dd f) -> p dd f", f=128)
            dst3 = wst2[:, 0:4096].rearrange("p (dd f) -> p dd f", f=128)
            tgt = p.dmat("sync", dst3, src3, sm["tr"])
            p.wait("sync", sm["ev"])
            tgs = p.dma("sync", red[:, 5:6],
                        d["ffn_gate_scale"][oi * 128:(oi + 1) * 128].rearrange("(p o) -> p o", o=1), sm["ldh"])
            tgs2 = p.dma("sync", red[:, 6:7],
                         d["ffn_up_scale"][oi * 128:(oi + 1) * 128].rearrange("(p o) -> p o", o=1), sm["ldh"])
            p.wait("vector", sm["ldh"], tgs2)
            p.op("vector", lambda e: e.tensor_scalar(red[:, 5:6], red[:, 5:6], 0.005, None, ALU.mult))
            p.drain("vector")
            p.op("vector", lambda e: e.tensor_scalar(red[:, 6:7], red[:, 6:7], 0.005, None, ALU.mult), sm["ev"])
            p.wait("gpsimd", sm["mm"], kth_free)
            p.wait("gpsimd", sm["lda"])
            tga = p.dma("gpsimd", kTh[:, 0:32], d["ffn_down_A"][oi * 128:(oi + 1) * 128, :], sm["lda"])
            p.wait("tensor", sm["tr"], tgt)
            for tc in range(TC):
                pb_ = pA if tc == 0 else pB
                pr0 = 64 * (oi % 2)
                p.op("tensor", lambda e, pb2=pb_, o2=oi, c2=tc, r0=pr0: e.matmul(
                    pb2[:, 0:512], bs_ap(BSG, o2),
                    scrb[r0:r0 + 33, c2 * 512:c2 * 512 + 512], start=True, stop=False))
                for dd in range(DT):
                    p.op("tensor", lambda e, pb2=pb_, d2=dd, c2=tc: e.matmul(
                        pb2[:, 0:512], wst2[:, d2 * 128:(d2 + 1) * 128],
                        y1T[:, d2 * T + c2 * 512: d2 * T + c2 * 512 + 512],
                        start=False, stop=(d2 == DT - 1)),
                        sem=sm["mm"] if dd == DT - 1 else None)
            mmg = sm["mm"].n
            p.wait("gpsimd", sm["tr"], tgt)
            ltu = p.dma("gpsimd", wst[:, 0:4096], d["ffn_up_q"][oi * 128:(oi + 1) * 128, :], sm["lw1"])
            p.wait("sync", sm["lw1"], ltu)
            p.wait("sync", sm["mm"], mmg)
            tgt2 = p.dmat("sync", dst3, src3, sm["tr"])
            wst_free = tgt2
            p.wait("tensor", sm["tr"], tgt2)
            for tc in range(TC):
                pb_ = pC if tc == 0 else pD
                pr0 = 64 * (oi % 2)
                p.op("tensor", lambda e, pb2=pb_, o2=oi, c2=tc, r0=pr0: e.matmul(
                    pb2[:, 0:512], bs_ap(BSU, o2),
                    qTh[r0:r0 + 33, c2 * 512:c2 * 512 + 512], start=True, stop=False))
                for dd in range(DT):
                    p.op("tensor", lambda e, pb2=pb_, d2=dd, c2=tc: e.matmul(
                        pb2[:, 0:512], wst2[:, d2 * 128:(d2 + 1) * 128],
                        y1T[:, d2 * T + c2 * 512: d2 * T + c2 * 512 + 512],
                        start=False, stop=(d2 == DT - 1)),
                        sem=sm["mm"] if dd == DT - 1 else None)
            mmu = sm["mm"].n
            for tc in range(TC):
                pg_ = pA if tc == 0 else pB
                pu_ = pC if tc == 0 else pD
                p.wait("vector", sm["mm"], mmu)
                p.wait("vector", sm["ev"])
                p.op("vector", lambda e, pg2=pg_: e.tensor_scalar(
                    scr[:, 0:512], pg2[:, 0:512], red[:, 5:6], None, ALU.mult), sm["ev"])
                p.wait("scalar", sm["ev"])
                p.op("scalar", lambda e: e.activation(scr[:, 512:1024], scr[:, 0:512], AF.Silu), sm["sc"])
                p.op("vector", lambda e, pu2=pu_: e.tensor_scalar(
                    scr[:, 1024:1536], pu2[:, 0:512], red[:, 6:7], None, ALU.mult))
                p.drain("vector")
                p.wait("vector", sm["sc"])
                p.op("vector", lambda e, o2=ol, c2=tc: e.tensor_mul(
                    ar2[:, o2 * T + c2 * 512: o2 * T + c2 * 512 + 512],
                    scr[:, 512:1024], scr[:, 1024:1536]), sm["ev"])
            evh = sm["ev"].n
            p.wait("tensor", sm["ev"], evh)
            p.wait("tensor", sm["lda"])
            for tc in range(TC if "t" not in skips else 0):
                pt_ = pE if tc == 0 else pG
                p.op("tensor", lambda e, pt2=pt_, o2=ol, c2=tc: e.matmul(
                    pt2[0:33, 0:512], kTh[:, 0:33],
                    ar2[:, o2 * T + c2 * 512: o2 * T + c2 * 512 + 512],
                    start=True, stop=True), sem=sm["mm"])
                mmt = sm["mm"].n
                p.wait("vector", sm["mm"], mmt)
                if qt == 0 and ol == 0:
                    p.op("vector", lambda e, pt2=pt_, c2=tc: e.tensor_copy(
                        ea_all[0:33, c2 * 512:c2 * 512 + 512], pt2[0:33, 0:512]), sm["ev"])
                else:
                    p.op("vector", lambda e, pt2=pt_, c2=tc: e.tensor_add(
                        ea_all[0:33, c2 * 512:c2 * 512 + 512],
                        ea_all[0:33, c2 * 512:c2 * 512 + 512], pt2[0:33, 0:512]), sm["ev"])
                evt = sm["ev"].n
                p.wait("tensor", sm["ev"], evt)
            kth_free = sm["mm"].n
        p.wait("sync", sm["ev"])
        dbg(f"hm_q{qt}", ar2[:, 0:qw * T], [128, qw * T], BF16)
        if qt == 3:
            for tc in range(TC):
                sl = slice(tc * 512, tc * 512 + 512)
                p.op("vector", lambda e, s=sl: e.tensor_scalar(
                    sqh[0:32, s], ea_all[0:32, s], 0.02, None, ALU.mult))
                p.drain("vector")
                p.op("vector", lambda e, s=sl: e.tensor_scalar(
                    sqh[64:96, s], ea_all[0:32, s], 0.02, None, ALU.mult))
                p.drain("vector")
                p.op("vector", lambda e, s=sl: e.tensor_copy(sqh[32:33, s], ea_all[32:33, s]))
                p.drain("vector")
                p.op("vector", lambda e, s=sl: e.tensor_copy(sqh[96:97, s], ea_all[32:33, s]), sm["ev"])
            p.wait("tensor", sm["ev"])
            p.wait("sync", sm["ev"])
            dbg("t3T", sqh[:], [128, T], BF16)
            dbg("rrow", skh[:], [128, TH], BF16)
            dbg("t1g", scrb[:], [128, 1024], BF16)
            dbg("t1u", qTh[:], [128, T], BF16)
        iw = qw * 128
        wst_free = sm["tr"].n
        for jglob in range(DT if "b" in parts else 0):
            j0 = jglob * 128
            p.wait("gpsimd", sm["tr"], wst_free)
            ltd = p.dma("gpsimd", wst[:, 0:iw],
                        d["ffn_down_q"][j0:j0 + 128, qb * 128: qb * 128 + iw], sm["lw1"])
            p.wait("sync", sm["lw1"], ltd)
            p.wait("sync", sm["mm"])
            src3 = wst[:, 0:iw].rearrange("p (it f) -> p it f", f=128)
            dst3 = wst2[:, 0:iw].rearrange("p (it f) -> p it f", f=128)
            tgt = p.dmat("sync", dst3, src3, sm["tr"])
            wst_free = tgt
            p.wait("tensor", sm["tr"], tgt)
            for tc in range(TC):
                pb_ = pA if tc == 0 else pB
                if qt == 3:
                    pr0 = 64 * (jglob % 2)
                    p.op("tensor", lambda e, pb2=pb_, jg2=jglob, c2=tc, r0=pr0: e.matmul(
                        pb2[:, 0:512], bs_ap(BSD, jg2),
                        sqh[r0:r0 + 33, c2 * 512:c2 * 512 + 512], start=True, stop=False))
                for ol in range(qw):
                    last = (ol == qw - 1)
                    p.op("tensor", lambda e, pb2=pb_, o2=ol, c2=tc, l2=last, s2=(ol == 0 and qt < 3): e.matmul(
                        pb2[:, 0:512], wst2[:, o2 * 128:(o2 + 1) * 128],
                        ar2[:, o2 * T + c2 * 512: o2 * T + c2 * 512 + 512],
                        start=s2, stop=l2),
                        sem=sm["mm"] if last else None)
            mmd = sm["mm"].n
            p.wait("vector", sm["mm"], mmd)
            for tc in range(TC):
                pb_ = pA if tc == 0 else pB
                if qt > 0:
                    tgl = p.dma("gpsimd", gs_h[:, tc * 512:tc * 512 + 512],
                                d["sp_yt"][j0:j0 + 128, tc * 512:tc * 512 + 512], sm["ldx"])
                    p.wait("vector", sm["ldx"], tgl)
                    p.op("vector", lambda e, pb2=pb_, c2=tc: e.tensor_add(
                        scr[:, 0:512], pb2[:, 0:512], gs_h[:, c2 * 512:c2 * 512 + 512]))
                    p.drain("vector")
                else:
                    p.op("vector", lambda e, pb2=pb_: e.tensor_copy(scr[:, 0:512], pb2[:, 0:512]))
                    p.drain("vector")
                if qt == 3:
                    tgs = p.dma("sync", red[:, 5:6],
                                d["ffn_down_scale"][j0:j0 + 128]
                                .rearrange("(p o) -> p o", o=1), sm["ldh"])
                    p.wait("vector", sm["ldh"], tgs)
                    p.op("vector", lambda e: e.tensor_scalar(red[:, 5:6], red[:, 5:6], 0.005, None, ALU.mult))
                    p.drain("vector")
                    p.op("vector", lambda e: e.tensor_scalar(scr[:, 0:512], scr[:, 0:512], red[:, 5:6], None, ALU.mult))
                    p.drain("vector")
                p.wait("vector", sm["st"])
                p.op("vector", lambda e, c2=tc: e.tensor_copy(
                    scr[:, 512 + c2 * 512:1024 + c2 * 512], scr[:, 0:512]), sm["ev"])
                p.drain("vector")
                evd = sm["ev"].n
                p.wait("sync", sm["ev"], evd)
                p.dma("sync", d["sp_yt"][j0:j0 + 128, tc * 512:tc * 512 + 512],
                      scr[:, 512 + tc * 512:1024 + tc * 512], sm["st"])
                p.wait("tensor", sm["ev"], evd)
            p.wait("gpsimd", sm["st"])

    _parts = globals().get("FFN_PARTS", "ab")
    _maxoi = globals().get("FFN_MAXOI", 99)
    _sk = globals().get("FFN_SKIP", "")
    for qt in range(4):
        if qt >= stop_after - 6:
            break
        ffn_quarter(qt, _parts, _maxoi, _sk)
        barrier()

    if stop_after < 11:
        meta = dict(nc=nc, ctx=ctx, prog=p, sems=sm, dram=d, sbuf=sb_, cst=cst)
        return meta
    for tt in range(NT):
        p.wait("sync", sm["st"])
        p.wait("gpsimd", sm["st"])
        tgh = p.dma("sync", ar2f[:, 0:4096], d["sp_y1"][tt * 128:(tt + 1) * 128, :], sm["ldh"])
        for rr in range(4):
            tg = p.dma("gpsimd",
                       ar2f[:, 4096 + rr * 1024: 4096 + (rr + 1) * 1024].rearrange("p (dt f) -> p dt f", f=128),
                       d["sp_yt"][rr * 1024:(rr + 1) * 1024, tt * 128:(tt + 1) * 128]
                       .rearrange("(dt p) f -> p dt f", p=128), sm["ldx"])
        p.wait("tensor", sm["ldx"], tg)
        p.wait("tensor", sm["ldh"], tgh)
        p.wait("tensor", sm["ev"])
        for q8 in range(8):
            for q4 in range(4):
                dtile = q8 * 4 + q4
                p.op("tensor", lambda e, d2=dtile, q2=q4: e.transpose(
                    pE[:, q2 * 128:(q2 + 1) * 128], ar2f[:, 4096 + d2 * 128: 4096 + (d2 + 1) * 128],
                    cst["ident_f"][:]), sem=sm["mm"] if q4 == 3 else None)
            mmq = sm["mm"].n
            p.wait("vector", sm["mm"], mmq)
            sl = ar2f[:, q8 * 512:(q8 + 1) * 512]
            p.op("vector", lambda e, s=sl: e.tensor_add(s, s, pE[:, 0:512]), sm["ev"])
            evq = sm["ev"].n
            p.wait("tensor", sm["ev"], evq)

        def emit_out(tt2=tt):
            p.wait("sync", sm["ev"])
            p.dma("sync", d["out"][tt2 * 128:(tt2 + 1) * 128, :], ar2f[:, 0:4096], sm["st"])
        ln_tile(ar2f[:, 0:4096], HID, emit_out)
        p.wait("gpsimd", sm["ev"])
    p.wait("sync", sm["st"])

    meta = dict(nc=nc, ctx=ctx, prog=p, sems=sm, dram=d, sbuf=sb_, cst=cst)
    return meta


def finalize(meta):
    nc, ctx, p = meta["nc"], meta["ctx"], meta["prog"]
    blk = ctx.enter_context(nc.Block())
    p.run(blk)
    ctx.close()
    return nc


def kernel(**inputs):
    meta = build()
    nc = finalize(meta)
    maps = shard_inputs(inputs)
    res = run_bass_kernel_spmd(nc, maps, core_ids=list(range(NCORES)))
    out = np.zeros((B, N, HID), dtype=np.float32)
    for core in range(NCORES):
        b, j = divmod(core, 4)
        out[b, j * T:(j + 1) * T] = np.asarray(res.results[core]["out"], dtype=np.float32)
    return out
